# version 33
# baseline (speedup 1.0000x reference)
import contextlib

import numpy as np
import ml_dtypes

import concourse.bass as bass
import concourse.mybir as mybir
from concourse.bass_utils import run_bass_kernel_spmd

F32 = mybir.dt.float32
BF16 = mybir.dt.bfloat16
I32 = mybir.dt.int32
AF = mybir.ActivationFunctionType
ALU = mybir.AluOpType
AX = mybir.AxisListType
BF = ml_dtypes.bfloat16

NCORES = 8
NSEQ = 2
SEQ = 2048
D = 1024
NT = SEQ // 128
HEADS = 16
NPAIR = 8
HFF = 3072
HCK = 512
NCK = HFF // HCK
EXP_SCALE = 0.125
EXP_SHIFT = -3.0
DEQ = 1.0 / 127.0

OFF_IDENT = 0
OFF_CONST = 512
OFF_STATS = 640
OFF_AE = 1024
OFF_E2C = 3072
OFF_RZT = 256
OFF_XTOK2 = 5120
OFF_XSTG = 7168
OFF_XN1 = 15360
OFF_GT = 15360
OFF_SG = 19456
OFF_HH = 21504
OFF_X1 = 25600
OFF_ATOK = 25600
OFF_PH0 = 58368
OFF_XNF = 91136
OFF_S2 = 123904
OFF_QP = 123904
OFF_KP = 132096
OFF_VP = 140288
OFF_WT0 = 123904
OFF_WT1 = 148480
OFF_WPOST = 173312
OFF_WP = 189696
OFF_AGELU = 189696
OFF_XNTOK = 206080
OFF_E2 = 208128


class Buf:

    __slots__ = ("name", "w", "r")

    def __init__(self, name):
        self.name = name
        self.w = {}
        self.r = {}


class Sch:
    ENGS = ("pe", "act", "dve", "sp", "gp")

    def __init__(self):
        self.plan = {e: [] for e in self.ENGS}
        self.cnt = {e: 0 for e in self.ENGS}
        self.waited = {e: {p: 0 for p in self.ENGS} for e in self.ENGS}

    def _op(self, eng, fn, waits, inc):
        wl = []
        if eng in ("sp", "gp") and self.cnt[eng] > 0:
            wl.append((eng, self.cnt[eng]))
        for (p, v) in waits:
            if p == eng and eng in ("pe", "sp", "gp"):
                continue
            if v > self.waited[eng][p]:
                self.waited[eng][p] = v
                wl.append((p, v))
        wl.sort()
        self.cnt[eng] += inc
        self.plan[eng].append((wl, fn, inc))
        return (eng, self.cnt[eng])

    def do(self, eng, fn, reads=(), writes=(), extra=(), inc=None):
        return self.group(eng, [fn], reads=reads, writes=writes, extra=extra,
                          inc=inc)

    def group(self, eng, fns, reads=(), writes=(), extra=(), inc=None):
        if inc is None:
            inc = 16 if eng in ("sp", "gp") else 1
        waits = list(extra)
        for b in reads:
            waits += list(b.w.items())
        for b in writes:
            waits += list(b.w.items())
            waits += list(b.r.items())
        tok = None
        for k, fn in enumerate(fns):
            tok = self._op(eng, fn, waits if k == 0 else (),
                           inc if k == len(fns) - 1 else 0)
        _, v = tok
        for b in reads:
            b.r[eng] = max(b.r.get(eng, 0), v)
        for b in writes:
            b.w[eng] = max(b.w.get(eng, 0), v)
        return tok


def build_program():
    nc = bass.Bass("TRN2", target_bir_lowering=False, debug=False)

    Xd = nc.dram_tensor("X", [NSEQ * SEQ, D], F32, kind="ExternalInput").ap()
    Wqd = nc.dram_tensor("Wq", [D, D], F32, kind="ExternalInput").ap()
    Wkd = nc.dram_tensor("Wk", [D, D], F32, kind="ExternalInput").ap()
    Wvd = nc.dram_tensor("Wv", [D, D], F32, kind="ExternalInput").ap()
    Wpd = nc.dram_tensor("Wp", [D, D], F32, kind="ExternalInput").ap()
    Wgd = nc.dram_tensor("Wg", [HFF, D], I32, kind="ExternalInput").ap()
    Wud = nc.dram_tensor("Wu", [HFF, D], I32, kind="ExternalInput").ap()
    Wdd = nc.dram_tensor("Wd", [D, HFF], I32, kind="ExternalInput").ap()
    Yd = nc.dram_tensor("Y", [NSEQ * SEQ, D], F32, kind="ExternalOutput").ap()

    wgTd = nc.dram_tensor("wgT_s", [NCK, 8, 128, HCK], BF16, kind="Internal").ap()
    wuTd = nc.dram_tensor("wuT_s", [NCK, 8, 128, HCK], BF16, kind="Internal").ap()
    wdTd = nc.dram_tensor("wdT_s", [NCK, 4, 128, D], BF16, kind="Internal").ap()
    identd = nc.inline_tensor(np.eye(128, dtype=BF), "identd").ap()

    _dummy = nc.alloc_sbuf_tensor("map_base", [128, 64], mybir.dt.uint8)
    _base = nc.lookup_mloc(_dummy).addr + 64

    A = lambda name, shape, dt, off: nc.alloc_sbuf_tensor_at(
        name, shape, dt, offset=_base + off)

    ident = A("ident", [128, 128], BF16, OFF_IDENT)
    consts = A("consts", [128, 16], F32, OFF_CONST)
    stats = A("stats", [128, 96], F32, OFF_STATS)
    ae = [A("ae0", [128, 512], BF16, OFF_AE),
          A("ae1", [128, 512], BF16, OFF_AE + 1024)]
    e2c = A("e2c", [128, 1024], BF16, OFF_E2C)
    rztmp = [A("rzt0", [128, 8], F32, OFF_RZT),
             A("rzt1", [128, 8], F32, OFF_RZT + 32)]
    xtok2 = A("xtok2", [128, 1024], BF16, OFF_XTOK2)
    xstg = A("xstg", [128, 2048], F32, OFF_XSTG)
    xn1 = A("xn1", [128, 1024], F32, OFF_XN1)
    gt = [A("gt0", [128, 1024], BF16, OFF_GT),
          A("gt1", [128, 1024], BF16, OFF_GT + 2048)]
    sg = A("sg", [128, 512], F32, OFF_SG)
    hh = A("hh", [128, 2048], BF16, OFF_HH)
    x1 = A("x1", [128, 16384], F32, OFF_X1)
    atok = A("atok", [128, 16384], BF16, OFF_ATOK)
    xnf = A("xnf", [128, 16384], BF16, OFF_XNF)
    qp = [A("qp0", [128, 2048], BF16, OFF_QP),
          A("qp1", [128, 2048], BF16, OFF_QP + 4096)]
    kp = [A("kp0", [128, 2048], BF16, OFF_KP),
          A("kp1", [128, 2048], BF16, OFF_KP + 4096)]
    vp = [A("vp0", [128, NT * 130], BF16, OFF_VP),
          A("vp1", [128, NT * 130], BF16, OFF_VP + 4160)]
    wpost = A("wpost", [128, 8 * D], BF16, OFF_WPOST)
    wqp = [A("wqp0", [128, 1024], BF16, OFF_WP),
           A("wqp1", [128, 1024], BF16, OFF_WP + 2048)]
    wkp = [A("wkp0", [128, 1024], BF16, OFF_WP + 4096),
           A("wkp1", [128, 1024], BF16, OFF_WP + 6144)]
    wvp = [A("wvp0", [128, 1024], BF16, OFF_WP + 8192),
           A("wvp1", [128, 1024], BF16, OFF_WP + 10240)]
    agelu = A("agelu", [128, 8192], BF16, OFF_AGELU)
    xtok = A("xtok", [128, 1024], BF16, OFF_XNTOK)
    xtoks = [xtok, xtok2]
    e2 = A("e2", [128, 2048], BF16, OFF_E2)
    p0_wg = A("p0_wg", [128, 2048], BF16, OFF_PH0)
    p0_wu = A("p0_wu", [128, 2048], BF16, OFF_PH0 + 4096)
    p0_wd = A("p0_wd", [128, 2048], BF16, OFF_PH0 + 8192)
    p0_gs = A("p0_gs", [128, 8 * 256], BF16, OFF_PH0 + 12288)
    p0_us = A("p0_us", [128, 8 * 256], BF16, OFF_PH0 + 16384)
    p0_ds = A("p0_ds", [128, 2 * 1024], BF16, OFF_PH0 + 20480)
    wgT = [A("wgT0", [128, 8 * HCK], BF16, OFF_WT0),
           A("wgT1", [128, 8 * HCK], BF16, OFF_WT1)]
    wuT = [A("wuT0", [128, 8 * HCK], BF16, OFF_WT0 + 8192),
           A("wuT1", [128, 8 * HCK], BF16, OFF_WT1 + 8192)]
    wdT = [A("wdT0", [128, 4 * D], BF16, OFF_WT0 + 16384),
           A("wdT1", [128, 4 * D], BF16, OFF_WT1 + 16384)]

    with contextlib.ExitStack() as stk:
        psA = stk.enter_context(nc.psum_tensor("psA", [128, 512], F32))
        psB = stk.enter_context(nc.psum_tensor("psB", [128, 512], F32))
        psC = stk.enter_context(nc.psum_tensor("psC", [128, 512], F32))
        psD = stk.enter_context(nc.psum_tensor("psD", [128, 512], F32))
        psE2 = stk.enter_context(nc.psum_tensor("psE2", [128, 1024], F32))
        psF2 = stk.enter_context(nc.psum_tensor("psF2", [128, 1024], F32))
        sems = {k: stk.enter_context(nc.semaphore("sem_" + k))
                for k in Sch.ENGS}
        block = stk.enter_context(nc.Block())

        S = Sch()
        B = {}
        for nm in ("ident", "consts", "stats",
                   "xn1", "sg", "atok", "xnf", "wpost",
                   "agelu", "psA", "psB", "psC", "psD", "psE2",
                   "psF2", "p0_wg", "p0_wu", "p0_wd", "p0_gs", "p0_us",
                   "p0_ds", "wgTd", "wuTd", "wdTd"):
            B[nm] = Buf(nm)
        B["ae"] = [Buf("ae0"), Buf("ae1")]
        B["rzt"] = [Buf("rzt0"), Buf("rzt1")]
        B["gt"] = [Buf("gt0"), Buf("gt1")]
        B["xtok"] = [Buf("xtok0"), Buf("xtok1")]
        B["qp"] = [Buf("qp0"), Buf("qp1")]
        B["kp"] = [Buf("kp0"), Buf("kp1")]
        B["vp"] = [Buf("vp0"), Buf("vp1")]
        B["wqp"] = [Buf("wqp0"), Buf("wqp1")]
        B["wkp"] = [Buf("wkp0"), Buf("wkp1")]
        B["wvp"] = [Buf("wvp0"), Buf("wvp1")]
        B["stP"] = [Buf("stP%d" % i) for i in range(NT)]
        B["stP2"] = [Buf("stP2_%d" % i) for i in range(4)]
        B["stS"] = [Buf("stS%d" % i) for i in range(4)]
        B["E2"] = [Buf("E2a"), Buf("E2b"), Buf("E2c")]
        e2sl = [(e2, 0), (e2, 1024), (e2c, 0)]
        B["xstg"] = [Buf("xstg0"), Buf("xstg1")]
        B["hh"] = [Buf("hh%d" % k) for k in range(4)]
        B["x1t"] = [Buf("x1_%d" % i) for i in range(NT)]
        B["wgT"] = [Buf("wgT0"), Buf("wgT1")]
        B["wuT"] = [Buf("wuT0"), Buf("wuT1")]
        B["wdT"] = [Buf("wdT0"), Buf("wdT1")]

        c_expsc = consts[:, 0:1]
        c_expsh = consts[:, 1:2]
        c_eps = consts[:, 2:3]
        c_mhalf = consts[:, 3:4]

        S.do("sp", lambda e: e.dma_start(out=ident[:, :], in_=identd),
             writes=[B["ident"]])
        for col, val in ((0, EXP_SCALE), (1, EXP_SHIFT), (2, 1e-5), (3, -0.5)):
            S.do("dve",
                 lambda e, col=col, val=val: e.memset(consts[:, col:col + 1], val),
                 writes=[B["consts"]])

        def phase0_items(hc, half):
            items = []
            r0 = hc * HCK + half * 256

            def loads():
                S.do("gp", lambda e: e.dma_start(
                    out=p0_wg[:, :].rearrange("p (s d) -> p s d", s=2),
                    in_=Wgd[r0:r0 + 256, :].rearrange("(s p) d -> p s d", p=128)),
                    writes=[B["p0_wg"]])
                S.do("gp", lambda e: e.dma_start(
                    out=p0_wu[:, :].rearrange("p (s d) -> p s d", s=2),
                    in_=Wud[r0:r0 + 256, :].rearrange("(s p) d -> p s d", p=128)),
                    writes=[B["p0_wu"]])
                S.do("gp", lambda e: e.dma_start(
                    out=p0_wd[:, :].rearrange("p (c h) -> p c h", c=8),
                    in_=Wdd.rearrange("(c p) h -> p c h", p=128)[:, :, r0:r0 + 256]),
                    writes=[B["p0_wd"]])
            items.append(loads)

            jobs = []
            for s in range(2):
                for c in range(8):
                    jobs.append((B["p0_wg"], p0_wg[:, s * 1024 + c * 128:
                                                   s * 1024 + (c + 1) * 128],
                                 B["p0_gs"],
                                 p0_gs[:, c * 256 + s * 128:c * 256 + (s + 1) * 128]))
            for s in range(2):
                for c in range(8):
                    jobs.append((B["p0_wu"], p0_wu[:, s * 1024 + c * 128:
                                                   s * 1024 + (c + 1) * 128],
                                 B["p0_us"],
                                 p0_us[:, c * 256 + s * 128:c * 256 + (s + 1) * 128]))
            for c in range(8):
                for s in range(2):
                    jobs.append((B["p0_wd"], p0_wd[:, c * 256 + s * 128:
                                                   c * 256 + (s + 1) * 128],
                                 B["p0_ds"],
                                 p0_ds[:, s * 1024 + c * 128:s * 1024 + (c + 1) * 128]))
            for k0 in range(0, len(jobs), 4):
                batch = jobs[k0:k0 + 4]
                ps_t, ps_tb = (psA, B["psA"]) if (k0 // 4) % 2 == 0 \
                    else (psB, B["psB"])

                def run_batch(batch=batch, ps_t=ps_t, ps_tb=ps_tb):
                    srcs = sorted({id(b): b for b, _, _, _ in batch}.values(),
                                  key=lambda b: b.name)
                    S.group("pe",
                            [(lambda e, src=src, sl=sl, ps_t=ps_t: e.matmul(
                                ps_t[:, sl * 128:(sl + 1) * 128], src,
                                ident[:, :], start=True, stop=True))
                             for sl, (_, src, _, _) in enumerate(batch)],
                            reads=[B["ident"]] + list(srcs), writes=[ps_tb])
                    dsts = sorted({id(b): b for _, _, b, _ in batch}.values(),
                                  key=lambda b: b.name)
                    S.group("dve",
                            [(lambda e, dst=dst, sl=sl, ps_t=ps_t:
                              e.tensor_copy(dst, ps_t[:, sl * 128:(sl + 1) * 128]))
                             for sl, (_, _, _, dst) in enumerate(batch)],
                            reads=[ps_tb], writes=list(dsts))
                items.append(run_batch)

            def stores():
                S.do("sp", lambda e: e.dma_start(
                    out=wgTd[hc].rearrange("c p h -> p c h")[:, :, half * 256:
                                                             half * 256 + 256],
                    in_=p0_gs[:, :].rearrange("p (c h) -> p c h", c=8)),
                    reads=[B["p0_gs"]], writes=[B["wgTd"]])
                S.do("sp", lambda e: e.dma_start(
                    out=wuTd[hc].rearrange("c p h -> p c h")[:, :, half * 256:
                                                             half * 256 + 256],
                    in_=p0_us[:, :].rearrange("p (c h) -> p c h", c=8)),
                    reads=[B["p0_us"]], writes=[B["wuTd"]])
                S.do("sp", lambda e: e.dma_start(
                    out=wdTd[hc].rearrange("s p d -> p s d")[:, half * 2:half * 2 + 2, :],
                    in_=p0_ds[:, :].rearrange("p (s d) -> p s d", s=2)),
                    reads=[B["p0_ds"]], writes=[B["wdTd"]])
            items.append(stores)
            return items

        def layernorm_tile(src_ap, src_buf, rstd_col, nmr_col, pbuf,
                           scr, sbuf_, deq=False, skip_sum=False):
            S.do("act", lambda e: e.activation(
                xn1[:, :], src_ap, AF.Square,
                accum_out=stats[:, scr + 1:scr + 2]),
                reads=[src_buf], writes=[B["xn1"], sbuf_])
            if not skip_sum:
                S.do("dve", lambda e: e.reduce_sum(
                    stats[:, scr:scr + 1], src_ap, axis=AX.X),
                    reads=[src_buf], writes=[sbuf_])
            S.do("dve", lambda e: e.tensor_scalar(
                stats[:, scr + 2:scr + 4], stats[:, scr:scr + 2],
                1.0 / D, None, ALU.mult),
                reads=[sbuf_], writes=[sbuf_])
            S.do("dve", lambda e: e.tensor_tensor(
                stats[:, scr + 4:scr + 5], stats[:, scr + 2:scr + 3],
                stats[:, scr + 2:scr + 3], ALU.mult),
                reads=[sbuf_], writes=[sbuf_])
            S.do("dve", lambda e: e.tensor_sub(
                stats[:, scr + 5:scr + 6], stats[:, scr + 3:scr + 4],
                stats[:, scr + 4:scr + 5]),
                reads=[sbuf_], writes=[sbuf_])
            S.do("act", lambda e: e.activation(
                stats[:, scr + 6:scr + 7], stats[:, scr + 5:scr + 6],
                AF.Ln, bias=c_eps),
                reads=[sbuf_, B["consts"]], writes=[sbuf_])
            S.do("act", lambda e: e.activation(
                stats[:, rstd_col:rstd_col + 1], stats[:, scr + 6:scr + 7],
                AF.Exp, scale=c_mhalf),
                reads=[sbuf_, B["consts"]], writes=[pbuf])
            if deq:
                S.do("dve", lambda e: e.tensor_scalar(
                    stats[:, rstd_col:rstd_col + 1],
                    stats[:, rstd_col:rstd_col + 1], DEQ, None, ALU.mult),
                    reads=[pbuf], writes=[pbuf])
            S.do("dve", lambda e: e.scalar_tensor_tensor(
                stats[:, nmr_col:nmr_col + 1], stats[:, scr + 2:scr + 3],
                -1.0, stats[:, rstd_col:rstd_col + 1], ALU.mult, ALU.mult),
                reads=[sbuf_, pbuf], writes=[pbuf])

        def transpose_to_featmajor(tok_t, tok_buf, dst, dst_buf, i,
                                   use_cd=False):
            if use_cd:
                S.group("pe",
                        [(lambda e, c=c: e.matmul(
                            (psC if c < 4 else psD)
                            [:, (c % 4) * 128:(c % 4 + 1) * 128],
                            tok_t[:, c * 128:(c + 1) * 128], ident[:, :],
                            start=True, stop=True)) for c in range(8)],
                        reads=[B["ident"], tok_buf],
                        writes=[B["psC"], B["psD"]])
                for pst, pstb, c0 in ((psC, B["psC"], 0), (psD, B["psD"], 4)):
                    S.do("dve", lambda e, pst=pst, c0=c0: e.tensor_copy(
                        dst[:, :].rearrange("p (c s) -> p c s", c=8)
                        [:, c0:c0 + 4, i * 128:(i + 1) * 128],
                        pst[:, :].rearrange("p (c s) -> p c s", c=4)),
                        reads=[pstb], writes=[dst_buf])
                return
            pst, pstb = (psE2, B["psE2"]) if i % 2 == 0 else (psF2, B["psF2"])
            S.group("pe",
                    [(lambda e, c=c, pst=pst: e.matmul(
                        pst[:, c * 128:(c + 1) * 128],
                        tok_t[:, c * 128:(c + 1) * 128], ident[:, :],
                        start=True, stop=True)) for c in range(8)],
                    reads=[B["ident"], tok_buf], writes=[pstb])
            S.do("dve", lambda e, pst=pst: e.tensor_copy(
                dst[:, :].rearrange("p (c s) -> p c s", c=8)[:, :, i * 128:
                                                            (i + 1) * 128],
                pst[:, :].rearrange("p (c s) -> p c s", c=8)),
                reads=[pstb], writes=[dst_buf])

        def alias_guard(dsts, srcs):
            for d in dsts:
                for sb in srcs:
                    for eng, v in list(sb.w.items()) + list(sb.r.items()):
                        d.r[eng] = max(d.r.get(eng, 0), v)

        bg = []

        def bg_drain(n=1):
            for _ in range(min(n, len(bg))):
                bg.pop(0)()

        def ln1_stats(row0, i):
            sl = i % 2
            xs = B["xstg"][sl]
            S.do("sp", lambda e, i=i, sl=sl, row0=row0: e.dma_start(
                out=xstg[:, sl * 1024:(sl + 1) * 1024],
                in_=Xd[row0 + i * 128:row0 + (i + 1) * 128, :]),
                writes=[xs])
            layernorm_tile(xstg[:, sl * 1024:(sl + 1) * 1024], xs,
                           2 * i, 2 * i + 1, B["stP"][i],
                           64 + 8 * (i % 2), B["stS"][i % 2])

        def ln1_fill_tile(row0, i, redma, use_cd=False):
            sl = i % 2
            xs = B["xstg"][sl]
            if redma:
                S.do("sp", lambda e, i=i, sl=sl, row0=row0: e.dma_start(
                    out=xstg[:, sl * 1024:(sl + 1) * 1024],
                    in_=Xd[row0 + i * 128:row0 + (i + 1) * 128, :]),
                    writes=[xs])
            tk = i % 2
            S.do("act", lambda e, i=i, sl=sl, tk=tk: e.activation(
                xtoks[tk][:, :], xstg[:, sl * 1024:(sl + 1) * 1024],
                AF.Identity,
                bias=stats[:, 2 * i + 1:2 * i + 2],
                scale=stats[:, 2 * i:2 * i + 1]),
                reads=[xs, B["stP"][i]], writes=[B["xtok"][tk]])
            transpose_to_featmajor(xtoks[tk], B["xtok"][tk], xnf,
                                   B["xnf"], i, use_cd=use_cd)

        kcell = [0]
        fill_done = [False]

        def sequence(s):
            row0 = s * SEQ

            wT_bufs = [B["wgT"][0], B["wgT"][1], B["wuT"][0], B["wuT"][1],
                       B["wdT"][0], B["wdT"][1]]
            pair_bufs = [B["qp"][0], B["qp"][1], B["kp"][0], B["kp"][1],
                         B["vp"][0], B["vp"][1]]
            pairw_bufs = [B["wqp"][0], B["wqp"][1], B["wkp"][0], B["wkp"][1],
                          B["wvp"][0], B["wvp"][1]]
            if s > 0:
                alias_guard([B["atok"]], B["x1t"][0:8])
            alias_guard(pair_bufs, wT_bufs)
            alias_guard(pairw_bufs, [B["agelu"]])
            for sl in range(2):
                S.do("dve", lambda e, sl=sl: e.memset(vp[sl][:, :], 1.0),
                     writes=[B["vp"][sl]])

            if s == 0:
                def wpost_load():
                    S.do("gp", lambda e: e.dma_start(
                        out=wpost[:, :].rearrange("p (c d) -> p c d", c=8),
                        in_=Wpd.rearrange("(c p) d -> p c d", p=128)),
                        writes=[B["wpost"]])
                bg.append(wpost_load)
                for hc in range(NCK):
                    bg.extend(phase0_items(hc, 0))
                    bg.extend(phase0_items(hc, 1))

                def ln1_head(i):
                    sl = i % 2
                    xs = B["xstg"][sl]
                    scr = 64 + 8 * (i % 4)
                    sb_ = B["stS"][i % 4]
                    src = xstg[:, sl * 1024:(sl + 1) * 1024]
                    S.do("sp", lambda e, i=i, sl=sl: e.dma_start(
                        out=xstg[:, sl * 1024:(sl + 1) * 1024],
                        in_=Xd[row0 + i * 128:row0 + (i + 1) * 128, :]),
                        writes=[xs])
                    S.do("act", lambda e, src=src, scr=scr: e.activation(
                        xn1[:, :], src, AF.Square,
                        accum_out=stats[:, scr + 1:scr + 2]),
                        reads=[xs], writes=[B["xn1"], sb_])
                    S.do("dve", lambda e, src=src, scr=scr: e.reduce_sum(
                        stats[:, scr:scr + 1], src, axis=AX.X),
                        reads=[xs], writes=[sb_])

                def ln1_tail(i):
                    scr = 64 + 8 * (i % 4)
                    sb_ = B["stS"][i % 4]
                    pbuf = B["stP"][i]
                    S.do("dve", lambda e, scr=scr: e.tensor_scalar(
                        stats[:, scr + 2:scr + 4], stats[:, scr:scr + 2],
                        1.0 / D, None, ALU.mult),
                        reads=[sb_], writes=[sb_])
                    S.do("dve", lambda e, scr=scr: e.tensor_tensor(
                        stats[:, scr + 4:scr + 5], stats[:, scr + 2:scr + 3],
                        stats[:, scr + 2:scr + 3], ALU.mult),
                        reads=[sb_], writes=[sb_])
                    S.do("dve", lambda e, scr=scr: e.tensor_sub(
                        stats[:, scr + 5:scr + 6], stats[:, scr + 3:scr + 4],
                        stats[:, scr + 4:scr + 5]),
                        reads=[sb_], writes=[sb_])
                    S.do("act", lambda e, scr=scr: e.activation(
                        stats[:, scr + 6:scr + 7], stats[:, scr + 5:scr + 6],
                        AF.Ln, bias=c_eps),
                        reads=[sb_, B["consts"]], writes=[sb_])
                    S.do("act", lambda e, scr=scr, i=i: e.activation(
                        stats[:, 2 * i:2 * i + 1], stats[:, scr + 6:scr + 7],
                        AF.Exp, scale=c_mhalf),
                        reads=[sb_, B["consts"]], writes=[pbuf])
                    S.do("dve", lambda e, scr=scr, i=i: e.scalar_tensor_tensor(
                        stats[:, 2 * i + 1:2 * i + 2],
                        stats[:, scr + 2:scr + 3], -1.0,
                        stats[:, 2 * i:2 * i + 1], ALU.mult, ALU.mult),
                        reads=[sb_, pbuf], writes=[pbuf])

                for i in range(NT):
                    ln1_head(i)
                    if i >= 1:
                        ln1_tail(i - 1)
                        ln1_fill_tile(row0, i - 1, redma=False)
                    bg_drain(2)
                ln1_tail(NT - 1)
                ln1_fill_tile(row0, NT - 1, redma=False)
            elif not fill_done[0]:
                for i in range(NT):
                    ln1_fill_tile(row0, i, redma=True)
            else:
                fill_done[0] = False

            def qkv_items(p):
                sl = p % 2
                items = []

                def loads(p=p, sl=sl):
                    for wd_ap, dst, dbuf in ((Wqd, wqp[sl], B["wqp"][sl]),
                                             (Wkd, wkp[sl], B["wkp"][sl]),
                                             (Wvd, wvp[sl], B["wvp"][sl])):
                        S.do("gp", lambda e, wd_ap=wd_ap, dst=dst, p=p:
                             e.dma_start(
                                 out=dst[:, :].rearrange("p (c n) -> p c n",
                                                         c=8),
                                 in_=wd_ap.rearrange("(c p) n -> p c n", p=128)
                                 [:, :, p * 128:(p + 1) * 128]),
                             writes=[dbuf])
                items.append(loads)

                for wi in range(2):
                    wsl_t = (wqp, wkp)[wi][sl]
                    wsl_b = (B["wqp"], B["wkp"])[wi][sl]
                    dstp = (qp, kp)[wi][sl]
                    dstb = (B["qp"], B["kp"])[wi][sl]
                    for jb in range(4):
                        ps, psb = (psA, B["psA"]) if (wi * 4 + jb) % 2 == 0 \
                            else (psB, B["psB"])

                        def qk_item(ps=ps, psb=psb, jb=jb, wsl_t=wsl_t,
                                    wsl_b=wsl_b, dstp=dstp, dstb=dstb):
                            S.group("pe",
                                    [(lambda e, c=c, ps=ps, jb=jb,
                                      wsl_t=wsl_t: e.matmul(
                                          ps[:, :],
                                          wsl_t[:, c * 128:(c + 1) * 128],
                                          xnf[:, c * SEQ + jb * 512:
                                              c * SEQ + (jb + 1) * 512],
                                          start=(c == 0), stop=(c == 7)))
                                     for c in range(8)],
                                    reads=[wsl_b, B["xnf"]], writes=[psb])
                            S.do("dve", lambda e, ps=ps, jb=jb, dstp=dstp:
                                 e.tensor_copy(
                                     dstp[:, jb * 512:(jb + 1) * 512],
                                     ps[:, :]),
                                 reads=[psb], writes=[dstb])
                        items.append(qk_item)

                for t in range(NT):
                    ps, psb = (psA, B["psA"]) if t % 2 == 0 else \
                        (psB, B["psB"])

                    def v_item(ps=ps, psb=psb, t=t, sl=sl):
                        S.group("pe",
                                [(lambda e, c=c, t=t, ps=ps, sl=sl: e.matmul(
                                    ps[:, 0:128],
                                    xnf[:, c * SEQ + t * 128:
                                        c * SEQ + (t + 1) * 128],
                                    wvp[sl][:, c * 128:(c + 1) * 128],
                                    start=(c == 0), stop=(c == 7)))
                                 for c in range(8)],
                                reads=[B["wvp"][sl], B["xnf"]], writes=[psb])
                        S.do("dve", lambda e, ps=ps, t=t, sl=sl: e.tensor_copy(
                            vp[sl][:, t * 130:(t + 1) * 130]
                            .rearrange("p (h v) -> p h v", v=65)[:, :, 0:64],
                            ps[:, 0:128].rearrange("p (h v) -> p h v", v=64)),
                            reads=[psb], writes=[B["vp"][sl]])
                    items.append(v_item)
                return items

            def emit_Sp(p, j, t, k):
                sl = p % 2
                sb, sbb = (psE2, B["psE2"]) if t % 2 == 0 else \
                    (psF2, B["psF2"])
                fns = []
                for half in range(2):
                    ro = half * 64
                    fns.append(lambda e, ro=ro, t=t, sb=sb, j=j, sl=sl:
                               e.matmul(
                                   sb[:, ro * 8:ro * 8 + 512],
                                   kp[sl][ro:ro + 64, t * 128:(t + 1) * 128],
                                   qp[sl][ro:ro + 64, j * 512:(j + 1) * 512],
                                   start=True, stop=True,
                                   tile_position=(ro, 0)))
                S.group("pe", fns, reads=[B["kp"][sl], B["qp"][sl]],
                        writes=[sbb])
                eten, eoff = e2sl[k % 3]
                S.do("act", lambda e, sb=sb, eten=eten, eoff=eoff:
                     e.activation(
                         eten[:, eoff:eoff + 1024], sb[:, :], AF.Exp,
                         bias=c_expsh, scale=c_expsc),
                     reads=[sbb, B["consts"]], writes=[B["E2"][k % 3]])

            def emit_AVp(p, j, t, k):
                sl = p % 2
                eten, eoff = e2sl[k % 3]
                esb = B["E2"][k % 3]
                for half in range(2):
                    at, atb = (psC, B["psC"]) if half == 0 else \
                        (psD, B["psD"])
                    S.do("pe", lambda e, t=t, half=half, at=at, eten=eten,
                         eoff=eoff, sl=sl: e.matmul(
                             at[0:65, :],
                             vp[sl][:, t * 130 + 65 * half:
                                    t * 130 + 65 * half + 65],
                             eten[:, eoff + half * 512:
                                  eoff + (half + 1) * 512],
                             start=(t == 0), stop=(t == NT - 1),
                             skip_group_check=True),
                         reads=[B["vp"][sl], esb], writes=[atb])

            def emit_epi_a(p, j):
                for half in range(2):
                    at, atb = (psC, B["psC"]) if half == 0 else \
                        (psD, B["psD"])
                    S.do("dve", lambda e, at=at, half=half: e.tensor_copy(
                        ae[half][0:65, :], at[0:65, :]),
                        reads=[atb], writes=[B["ae"][half]])

            def emit_epi_b(p, j):
                for half in range(2):
                    hl = 2 * p + half
                    ps_t, ps_tb = (psA, B["psA"]) if half == 0 else \
                        (psB, B["psB"])
                    ae_s, ae_b = ae[half], B["ae"][half]
                    rzs, rzb = rztmp[half], B["rzt"][half]
                    S.group("pe",
                            [(lambda e, qt=qt, ae_s=ae_s, ps_t=ps_t:
                              e.matmul(
                                  ps_t[:, qt * 65:(qt + 1) * 65],
                                  ae_s[0:65, qt * 128:(qt + 1) * 128],
                                  ident[0:65, 0:65],
                                  start=True, stop=True))
                             for qt in range(4)],
                            reads=[ae_b, B["ident"]], writes=[ps_tb])
                    S.do("dve", lambda e, ps_t=ps_t, rzs=rzs:
                         e.reciprocal(
                             rzs[:, 0:4].rearrange("p (i o) -> p i o", o=1),
                             ps_t[:, 0:260].rearrange("p (i o) -> p i o",
                                                      o=65)[:, :, 64:65]),
                         reads=[ps_tb], writes=[rzb])
                    for qt in range(4):
                        i = j * 4 + qt
                        S.do("dve", lambda e, qt=qt, i=i, hl=hl,
                             ps_t=ps_t, rzs=rzs: e.tensor_scalar(
                                 atok[:, i * 1024 + hl * 64:
                                      i * 1024 + (hl + 1) * 64],
                                 ps_t[:, qt * 65:qt * 65 + 64],
                                 rzs[:, qt:qt + 1], None, ALU.mult),
                             reads=[ps_tb, rzb], writes=[B["atok"]])

            for it in qkv_items(0):
                it()

            seqk = [(p, j, t) for p in range(NPAIR) for j in range(4)
                    for t in range(NT)]
            emit_Sp(*seqk[0], kcell[0])
            emit_Sp(*seqk[1], kcell[0] + 1)
            pend = []
            for k, (p, j, t) in enumerate(seqk):
                if j == 0 and t == 0 and p + 1 < NPAIR:
                    bg[:0] = qkv_items(p + 1)
                if k + 2 < len(seqk):
                    emit_Sp(*seqk[k + 2], kcell[0] + k + 2)
                while pend and pend[0][0] <= k:
                    pend.pop(0)[1]()
                bg_drain(1)
                emit_AVp(p, j, t, kcell[0] + k)
                if t == NT - 1:
                    pend.append((k + 1,
                                 lambda pp=p, pj=j: emit_epi_a(pp, pj)))
                    pend.append((k + 2,
                                 lambda pp=p, pj=j: emit_epi_b(pp, pj)))
            while pend:
                pend.pop(0)[1]()
            kcell[0] += len(seqk)

            alias_guard(B["x1t"][8:14],
                        [B["p0_wg"], B["p0_wu"], B["p0_wd"],
                         B["p0_gs"], B["p0_us"], B["p0_ds"]])
            alias_guard([B["agelu"]], pairw_bufs)
            ln2q = []
            for half in (1, 0):
                alias_guard([B["gt"][0], B["gt"][1]], [B["xn1"]])
                for it in range(8):
                    i = half * 8 + it
                    sl = it % 2
                    gts, gtb = gt[sl], B["gt"][sl]
                    bank0, bank0b, bank1, bank1b = \
                        (psA, B["psA"], psB, B["psB"]) if it % 2 == 0 else \
                        (psC, B["psC"], psD, B["psD"])
                    S.do("act", lambda e, gts=gts, i=i: e.activation(
                        gts[:, :], atok[:, i * 1024:(i + 1) * 1024], AF.Gelu),
                        reads=[B["atok"]], writes=[gtb])
                    S.group("pe",
                            [(lambda e, b=b, gts=gts, bank0=bank0,
                              bank1=bank1: e.matmul(
                                  (bank0 if b < 4 else bank1)
                                  [:, (b % 4) * 128:(b % 4 + 1) * 128],
                                  gts[:, b * 128:(b + 1) * 128], ident[:, :],
                                  start=True, stop=True)) for b in range(8)],
                            reads=[gtb, B["ident"]],
                            writes=[bank0b, bank1b])
                    for ps_t, ps_tb, c0 in ((bank0, bank0b, 0),
                                            (bank1, bank1b, 4)):
                        S.do("dve", lambda e, ps_t=ps_t, c0=c0, it=it:
                             e.tensor_copy(
                                 agelu[:, :].rearrange("p (c q) -> p c q", c=8)
                                 [:, c0:c0 + 4, it * 128:(it + 1) * 128],
                                 ps_t[:, :].rearrange("p (c q) -> p c q", c=4)),
                             reads=[ps_tb], writes=[B["agelu"]])
                if half == 0:
                    alias_guard(B["x1t"][0:8], [B["atok"]])
                alias_guard([B["xn1"]], [B["gt"][0], B["gt"][1]])
                for it in range(8):
                    i = half * 8 + it
                    sl = i % 2
                    xs = B["xstg"][sl]
                    S.do("sp", lambda e, i=i, sl=sl: e.dma_start(
                        out=xstg[:, sl * 1024:(sl + 1) * 1024],
                        in_=Xd[row0 + i * 128:row0 + (i + 1) * 128, :]),
                        writes=[xs])
                    S.do("dve", lambda e, sl=sl, i=i: e.scalar_tensor_tensor(
                        x1[:, i * 1024:(i + 1) * 1024],
                        xstg[:, sl * 1024:(sl + 1) * 1024],
                        stats[:, 2 * i:2 * i + 1],
                        xstg[:, sl * 1024:(sl + 1) * 1024],
                        ALU.mult, ALU.add),
                        reads=[xs, B["stP"][i]], writes=[B["x1t"][i]])
                    ps, psb = (psE2, B["psE2"]) if it % 2 == 0 else (psF2, B["psF2"])
                    fns = []
                    for nb in range(2):
                        for c in range(8):
                            fns.append(lambda e, c=c, nb=nb, it=it, ps=ps:
                                       e.matmul(
                                           ps[:, nb * 512:(nb + 1) * 512],
                                           agelu[:, c * 1024 + it * 128:
                                                 c * 1024 + (it + 1) * 128],
                                           wpost[:, c * D + nb * 512:
                                                 c * D + (nb + 1) * 512],
                                           start=(c == 0), stop=(c == 7)))
                    S.group("pe", fns, reads=[B["agelu"], B["wpost"]],
                            writes=[psb])
                    scr2 = 80 + 8 * (i % 2)
                    S.do("dve", lambda e, i=i, ps=ps, scr2=scr2:
                         e.scalar_tensor_tensor(
                             x1[:, i * 1024:(i + 1) * 1024],
                             x1[:, i * 1024:(i + 1) * 1024],
                             stats[:, 2 * i + 1:2 * i + 2], ps[:, :],
                             ALU.add, ALU.add,
                             accum_out=stats[:, scr2:scr2 + 1]),
                         reads=[psb, B["stP"][i], B["x1t"][i]],
                         writes=[B["x1t"][i], B["stS"][2 + i % 2]])
                    while len(ln2q) >= 2:
                        ln2q.pop(0)()
                    sl4 = i % 4
                    layernorm_tile(x1[:, i * 1024:(i + 1) * 1024],
                                   B["x1t"][i], 32 + 2 * sl4, 33 + 2 * sl4,
                                   B["stP2"][sl4], 80 + 8 * (i % 2),
                                   B["stS"][2 + i % 2], deq=True,
                                   skip_sum=True)

                    def ln2_tail(i=i, sl4=sl4):
                        tk = i % 2
                        S.do("act", lambda e, i=i, sl4=sl4, tk=tk:
                             e.activation(
                                 xtoks[tk][:, :],
                                 x1[:, i * 1024:(i + 1) * 1024],
                                 AF.Identity,
                                 bias=stats[:, 33 + 2 * sl4:34 + 2 * sl4],
                                 scale=stats[:, 32 + 2 * sl4:33 + 2 * sl4]),
                             reads=[B["x1t"][i], B["stP2"][sl4]],
                             writes=[B["xtok"][tk]])
                        transpose_to_featmajor(xtoks[tk], B["xtok"][tk],
                                               xnf, B["xnf"], i,
                                               use_cd=True)
                    ln2q.append(ln2_tail)
            while ln2q:
                ln2q.pop(0)()

            if s + 1 < NSEQ:
                bg.extend([lambda i=i: ln1_stats((s + 1) * SEQ, i)
                           for i in range(NT)])
            alias_guard(wT_bufs, pair_bufs)
            for hc in range(NCK):
                ws = hc % 2
                for src_d, dst_sb, dbuf, nchunks in (
                        (wgTd, wgT[ws], B["wgT"][ws], 8),
                        (wuTd, wuT[ws], B["wuT"][ws], 8),
                        (wdTd, wdT[ws], B["wdT"][ws], 4)):
                    S.do("gp", lambda e, src_d=src_d, dst_sb=dst_sb, hc=hc,
                         nchunks=nchunks: e.dma_start(
                             out=dst_sb[:, :].rearrange("p (c h) -> p c h",
                                                        c=nchunks),
                             in_=src_d[hc].rearrange("c p h -> p c h")),
                         reads=[B["wgTd" if src_d is wgTd else
                                  "wuTd" if src_d is wuTd else "wdTd"]],
                         writes=[dbuf])
                for j in range(4):
                    bg_drain(1)
                    for hsub in range(4):
                        psg, psgb = (psA, B["psA"]) if hsub % 2 == 0 else (psB, B["psB"])
                        psu, psub = (psC, B["psC"]) if hsub % 2 == 0 else (psD, B["psD"])
                        S.group("pe",
                                [(lambda e, c=c, hsub=hsub, j=j, ws=ws, psg=psg:
                                  e.matmul(
                                      psg[:, :],
                                      wgT[ws][:, c * 512 + hsub * 128:
                                              c * 512 + (hsub + 1) * 128],
                                      xnf[:, c * SEQ + j * 512:
                                          c * SEQ + (j + 1) * 512],
                                      start=(c == 0), stop=(c == 7)))
                                 for c in range(8)],
                                reads=[B["wgT"][ws], B["xnf"]], writes=[psgb])
                        S.group("pe",
                                [(lambda e, c=c, hsub=hsub, j=j, ws=ws, psu=psu:
                                  e.matmul(
                                      psu[:, :],
                                      wuT[ws][:, c * 512 + hsub * 128:
                                              c * 512 + (hsub + 1) * 128],
                                      xnf[:, c * SEQ + j * 512:
                                          c * SEQ + (j + 1) * 512],
                                      start=(c == 0), stop=(c == 7)))
                                 for c in range(8)],
                                reads=[B["wuT"][ws], B["xnf"]], writes=[psub])
                        S.do("act", lambda e, psg=psg: e.activation(
                            sg[:, :], psg[:, :], AF.Silu),
                            reads=[psgb], writes=[B["sg"]])
                        S.do("dve", lambda e, psu=psu, hsub=hsub:
                             e.scalar_tensor_tensor(
                                 hh[:, hsub * 512:(hsub + 1) * 512],
                                 psu[:, :], DEQ, sg[:, :], ALU.mult, ALU.mult),
                             reads=[psub, B["sg"]], writes=[B["hh"][hsub]])
                    if hc == NCK - 1 and s + 1 < NSEQ and j >= 1:
                        for fi in range(4 * (j - 1), 4 * j):
                            ln1_fill_tile((s + 1) * SEQ, fi, redma=True,
                                          use_cd=True)
                    for it in range(4):
                        i = j * 4 + it
                        psd_, psdb = (psE2, B["psE2"]) if it % 2 == 0 \
                            else (psF2, B["psF2"])
                        fns = []
                        for hsub in range(4):
                            for nb in range(2):
                                fns.append(lambda e, hsub=hsub, nb=nb, it=it,
                                           ws=ws, psd_=psd_: e.matmul(
                                               psd_[:, nb * 512:(nb + 1) * 512],
                                               hh[:, hsub * 512 + it * 128:
                                                  hsub * 512 + (it + 1) * 128],
                                               wdT[ws][:, hsub * D + nb * 512:
                                                       hsub * D + (nb + 1) * 512],
                                               start=(hsub == 0), stop=(hsub == 3)))
                        S.group("pe", fns,
                                reads=[B["hh"][0], B["hh"][1], B["hh"][2],
                                       B["hh"][3], B["wdT"][ws]],
                                writes=[psdb])
                        S.do("dve", lambda e, i=i, psd_=psd_: e.tensor_add(
                            x1[:, i * 1024:(i + 1) * 1024],
                            x1[:, i * 1024:(i + 1) * 1024], psd_[:, :]),
                            reads=[psdb, B["x1t"][i]], writes=[B["x1t"][i]])
                        if hc == NCK - 1:
                            S.do("act", lambda e, i=i: e.dma_start(
                                out=Yd[row0 + i * 128:row0 + (i + 1) * 128, :],
                                in_=x1[:, i * 1024:(i + 1) * 1024]),
                                reads=[B["x1t"][i]], inc=16)

            if s + 1 < NSEQ:
                for fi in range(12, NT):
                    ln1_fill_tile((s + 1) * SEQ, fi, redma=True,
                                  use_cd=True)
                fill_done[0] = True

        for s in range(NSEQ):
            sequence(s)
        assert not bg, "background queue not fully drained"

        for q in ("sp", "gp", "act"):
            v = S.cnt[q]
            S.plan[q].append(
                ([], lambda e, q=q, v=v: e.wait_ge(sems[q], v), 0))

        def emit(ename):
            plan = S.plan[ename]

            def emitter(e):
                for wl, fn, inc in plan:
                    for (p, v) in wl:
                        e.wait_ge(sems[p], v)
                    ins = fn(e)
                    if inc:
                        ins.then_inc(sems[ename], inc)
            return emitter

        block.sync(emit("sp"))
        block.gpsimd(emit("gp"))
        block.tensor(emit("pe"))
        block.scalar(emit("act"))
        block.vector(emit("dve"))

    return nc


_NC_CACHE = {}


def kernel(L0_attn_X, L0_attn_Wq, L0_attn_Wk, L0_attn_Wv, L0_attn_Wpost,
           L0_ffn_w_gate, L0_ffn_w_up, L0_ffn_w_down):
    X = np.ascontiguousarray(np.asarray(L0_attn_X), dtype=np.float32)
    Wq = np.ascontiguousarray(np.asarray(L0_attn_Wq), dtype=np.float32)
    Wk = np.ascontiguousarray(np.asarray(L0_attn_Wk), dtype=np.float32)
    Wv = np.ascontiguousarray(np.asarray(L0_attn_Wv), dtype=np.float32)
    Wp = np.ascontiguousarray(np.asarray(L0_attn_Wpost), dtype=np.float32)
    Wg = np.ascontiguousarray(np.asarray(L0_ffn_w_gate), dtype=np.int32)
    Wu = np.ascontiguousarray(np.asarray(L0_ffn_w_up), dtype=np.int32)
    Wd = np.ascontiguousarray(np.asarray(L0_ffn_w_down), dtype=np.int32)

    if "nc" not in _NC_CACHE:
        _NC_CACHE["nc"] = build_program()
    nc = _NC_CACHE["nc"]

    in_maps = []
    for c in range(NCORES):
        xs = X[c * NSEQ:(c + 1) * NSEQ].reshape(NSEQ * SEQ, D)
        in_maps.append({"X": np.ascontiguousarray(xs), "Wq": Wq, "Wk": Wk,
                        "Wv": Wv, "Wp": Wp, "Wg": Wg, "Wu": Wu, "Wd": Wd})
    res = run_bass_kernel_spmd(nc, in_maps, core_ids=list(range(NCORES)))
    out = np.concatenate(
        [res.results[c]["Y"].reshape(NSEQ, SEQ, D) for c in range(NCORES)],
        axis=0)
    return out


# version 34
# speedup vs baseline: 1.0304x; 1.0304x over previous
import contextlib

import numpy as np
import ml_dtypes

import concourse.bass as bass
import concourse.mybir as mybir
from concourse.bass_utils import run_bass_kernel_spmd

F32 = mybir.dt.float32
BF16 = mybir.dt.bfloat16
I32 = mybir.dt.int32
AF = mybir.ActivationFunctionType
ALU = mybir.AluOpType
AX = mybir.AxisListType
BF = ml_dtypes.bfloat16

NCORES = 8
NSEQ = 2
SEQ = 2048
D = 1024
NT = SEQ // 128
HEADS = 16
NPAIR = 8
HFF = 3072
HCK = 512
NCK = HFF // HCK
EXP_SCALE = 0.125
EXP_SHIFT = -3.0
DEQ = 1.0 / 127.0

OFF_IDENT = 0
OFF_CONST = 512
OFF_STATS = 640
OFF_AE = 1024
OFF_E2C = 3072
OFF_RZT = 256
OFF_XTOK2 = 5120
OFF_XSTG = 7168
OFF_XN1 = 15360
OFF_GT = 15360
OFF_SG = 19456
OFF_HH = 21504
OFF_X1 = 25600
OFF_ATOK = 25600
OFF_PH0 = 58368
OFF_XNF = 91136
OFF_S2 = 123904
OFF_QP = 123904
OFF_KP = 132096
OFF_VP = 140288
OFF_WT0 = 123904
OFF_WT1 = 148480
OFF_WPOST = 173312
OFF_WP = 189696
OFF_AGELU = 189696
OFF_XNTOK = 206080
OFF_E2 = 208128


class Buf:

    __slots__ = ("name", "w", "r")

    def __init__(self, name):
        self.name = name
        self.w = {}
        self.r = {}


class Sch:
    ENGS = ("pe", "act", "dve", "sp", "gp")

    def __init__(self):
        self.plan = {e: [] for e in self.ENGS}
        self.cnt = {e: 0 for e in self.ENGS}
        self.waited = {e: {p: 0 for p in self.ENGS} for e in self.ENGS}

    def _op(self, eng, fn, waits, inc):
        wl = []
        if eng in ("sp", "gp") and self.cnt[eng] > 0:
            wl.append((eng, self.cnt[eng]))
        for (p, v) in waits:
            if p == eng and eng in ("pe", "sp", "gp"):
                continue
            if v > self.waited[eng][p]:
                self.waited[eng][p] = v
                wl.append((p, v))
        wl.sort()
        self.cnt[eng] += inc
        self.plan[eng].append((wl, fn, inc))
        return (eng, self.cnt[eng])

    def do(self, eng, fn, reads=(), writes=(), extra=(), inc=None):
        return self.group(eng, [fn], reads=reads, writes=writes, extra=extra,
                          inc=inc)

    def group(self, eng, fns, reads=(), writes=(), extra=(), inc=None):
        if inc is None:
            inc = 16 if eng in ("sp", "gp") else 1
        waits = list(extra)
        for b in reads:
            waits += list(b.w.items())
        for b in writes:
            waits += list(b.w.items())
            waits += list(b.r.items())
        tok = None
        for k, fn in enumerate(fns):
            tok = self._op(eng, fn, waits if k == 0 else (),
                           inc if k == len(fns) - 1 else 0)
        _, v = tok
        for b in reads:
            b.r[eng] = max(b.r.get(eng, 0), v)
        for b in writes:
            b.w[eng] = max(b.w.get(eng, 0), v)
        return tok


def build_program():
    nc = bass.Bass("TRN2", target_bir_lowering=False, debug=False)

    Xd = nc.dram_tensor("X", [NSEQ * SEQ, D], F32, kind="ExternalInput").ap()
    Wqd = nc.dram_tensor("Wq", [D, D], F32, kind="ExternalInput").ap()
    Wkd = nc.dram_tensor("Wk", [D, D], F32, kind="ExternalInput").ap()
    Wvd = nc.dram_tensor("Wv", [D, D], F32, kind="ExternalInput").ap()
    Wpd = nc.dram_tensor("Wp", [D, D], F32, kind="ExternalInput").ap()
    Wgd = nc.dram_tensor("Wg", [HFF, D], I32, kind="ExternalInput").ap()
    Wud = nc.dram_tensor("Wu", [HFF, D], I32, kind="ExternalInput").ap()
    Wdd = nc.dram_tensor("Wd", [D, HFF], I32, kind="ExternalInput").ap()
    Yd = nc.dram_tensor("Y", [NSEQ * SEQ, D], F32, kind="ExternalOutput").ap()

    wgTd = nc.dram_tensor("wgT_s", [NCK, 8, 128, HCK], BF16, kind="Internal").ap()
    wuTd = nc.dram_tensor("wuT_s", [NCK, 8, 128, HCK], BF16, kind="Internal").ap()
    wdTd = nc.dram_tensor("wdT_s", [NCK, 4, 128, D], BF16, kind="Internal").ap()
    identd = nc.inline_tensor(np.eye(128, dtype=BF), "identd").ap()

    _dummy = nc.alloc_sbuf_tensor("map_base", [128, 64], mybir.dt.uint8)
    _base = nc.lookup_mloc(_dummy).addr + 64

    A = lambda name, shape, dt, off: nc.alloc_sbuf_tensor_at(
        name, shape, dt, offset=_base + off)

    ident = A("ident", [128, 128], BF16, OFF_IDENT)
    consts = A("consts", [128, 16], F32, OFF_CONST)
    stats = A("stats", [128, 96], F32, OFF_STATS)
    ae = [A("ae0", [128, 512], BF16, OFF_AE),
          A("ae1", [128, 512], BF16, OFF_AE + 1024)]
    e2c = A("e2c", [128, 1024], BF16, OFF_E2C)
    rztmp = [A("rzt0", [128, 8], F32, OFF_RZT),
             A("rzt1", [128, 8], F32, OFF_RZT + 32)]
    xtok2 = A("xtok2", [128, 1024], BF16, OFF_XTOK2)
    xstg = A("xstg", [128, 2048], F32, OFF_XSTG)
    xn1 = A("xn1", [128, 1024], F32, OFF_XN1)
    gt = [A("gt0", [128, 1024], BF16, OFF_GT),
          A("gt1", [128, 1024], BF16, OFF_GT + 2048)]
    sg = A("sg", [128, 512], F32, OFF_SG)
    hh = A("hh", [128, 2048], BF16, OFF_HH)
    x1 = A("x1", [128, 16384], F32, OFF_X1)
    atok = A("atok", [128, 16384], BF16, OFF_ATOK)
    xnf = A("xnf", [128, 16384], BF16, OFF_XNF)
    qp = [A("qp0", [128, 2048], BF16, OFF_QP),
          A("qp1", [128, 2048], BF16, OFF_QP + 4096)]
    kp = [A("kp0", [128, 2048], BF16, OFF_KP),
          A("kp1", [128, 2048], BF16, OFF_KP + 4096)]
    vp = [A("vp0", [128, NT * 130], BF16, OFF_VP),
          A("vp1", [128, NT * 130], BF16, OFF_VP + 4160)]
    wpost = A("wpost", [128, 8 * D], BF16, OFF_WPOST)
    wqp = [A("wqp0", [128, 1024], BF16, OFF_WP),
           A("wqp1", [128, 1024], BF16, OFF_WP + 2048)]
    wkp = [A("wkp0", [128, 1024], BF16, OFF_WP + 4096),
           A("wkp1", [128, 1024], BF16, OFF_WP + 6144)]
    wvp = [A("wvp0", [128, 1024], BF16, OFF_WP + 8192),
           A("wvp1", [128, 1024], BF16, OFF_WP + 10240)]
    agelu = A("agelu", [128, 8192], BF16, OFF_AGELU)
    xtok = A("xtok", [128, 1024], BF16, OFF_XNTOK)
    xtoks = [xtok, xtok2]
    e2 = A("e2", [128, 2048], BF16, OFF_E2)
    p0_wg = A("p0_wg", [128, 2048], BF16, OFF_PH0)
    p0_wu = A("p0_wu", [128, 2048], BF16, OFF_PH0 + 4096)
    p0_wd = A("p0_wd", [128, 2048], BF16, OFF_PH0 + 8192)
    p0_gs = A("p0_gs", [128, 8 * 256], BF16, OFF_PH0 + 12288)
    p0_us = A("p0_us", [128, 8 * 256], BF16, OFF_PH0 + 16384)
    p0_ds = A("p0_ds", [128, 2 * 1024], BF16, OFF_PH0 + 20480)
    wgT = [A("wgT0", [128, 8 * HCK], BF16, OFF_WT0),
           A("wgT1", [128, 8 * HCK], BF16, OFF_WT1)]
    wuT = [A("wuT0", [128, 8 * HCK], BF16, OFF_WT0 + 8192),
           A("wuT1", [128, 8 * HCK], BF16, OFF_WT1 + 8192)]
    wdT = [A("wdT0", [128, 4 * D], BF16, OFF_WT0 + 16384),
           A("wdT1", [128, 4 * D], BF16, OFF_WT1 + 16384)]

    with contextlib.ExitStack() as stk:
        psA = stk.enter_context(nc.psum_tensor("psA", [128, 512], F32))
        psB = stk.enter_context(nc.psum_tensor("psB", [128, 512], F32))
        psC = stk.enter_context(nc.psum_tensor("psC", [128, 512], F32))
        psD = stk.enter_context(nc.psum_tensor("psD", [128, 512], F32))
        psE2 = stk.enter_context(nc.psum_tensor("psE2", [128, 1024], F32))
        psF2 = stk.enter_context(nc.psum_tensor("psF2", [128, 1024], F32))
        sems = {k: stk.enter_context(nc.semaphore("sem_" + k))
                for k in Sch.ENGS}
        block = stk.enter_context(nc.Block())

        S = Sch()
        B = {}
        for nm in ("ident", "consts", "stats",
                   "xn1", "sg", "atok", "xnf", "wpost",
                   "agelu", "psA", "psB", "psC", "psD", "psE2",
                   "psF2", "p0_wg", "p0_wu", "p0_wd", "p0_gs", "p0_us",
                   "p0_ds", "wgTd", "wuTd", "wdTd"):
            B[nm] = Buf(nm)
        B["ae"] = [Buf("ae0"), Buf("ae1")]
        B["rzt"] = [Buf("rzt0"), Buf("rzt1")]
        B["gt"] = [Buf("gt0"), Buf("gt1")]
        B["xtok"] = [Buf("xtok0"), Buf("xtok1")]
        B["qp"] = [Buf("qp0"), Buf("qp1")]
        B["kp"] = [Buf("kp0"), Buf("kp1")]
        B["vp"] = [Buf("vp0"), Buf("vp1")]
        B["wqp"] = [Buf("wqp0"), Buf("wqp1")]
        B["wkp"] = [Buf("wkp0"), Buf("wkp1")]
        B["wvp"] = [Buf("wvp0"), Buf("wvp1")]
        B["stP"] = [Buf("stP%d" % i) for i in range(NT)]
        B["stP2"] = [Buf("stP2_%d" % i) for i in range(4)]
        B["stS"] = [Buf("stS%d" % i) for i in range(4)]
        B["E2"] = [Buf("E2a"), Buf("E2b"), Buf("E2c")]
        e2sl = [(e2, 0), (e2, 1024), (e2c, 0)]
        B["xstg"] = [Buf("xstg0"), Buf("xstg1")]
        B["hh"] = [Buf("hh%d" % k) for k in range(4)]
        B["x1t"] = [Buf("x1_%d" % i) for i in range(NT)]
        B["wgT"] = [Buf("wgT0"), Buf("wgT1")]
        B["wuT"] = [Buf("wuT0"), Buf("wuT1")]
        B["wdT"] = [Buf("wdT0"), Buf("wdT1")]

        c_expsc = consts[:, 0:1]
        c_expsh = consts[:, 1:2]
        c_eps = consts[:, 2:3]
        c_mhalf = consts[:, 3:4]

        S.do("sp", lambda e: e.dma_start(out=ident[:, :], in_=identd),
             writes=[B["ident"]])
        for col, val in ((0, EXP_SCALE), (1, EXP_SHIFT), (2, 1e-5), (3, -0.5)):
            S.do("dve",
                 lambda e, col=col, val=val: e.memset(consts[:, col:col + 1], val),
                 writes=[B["consts"]])

        def phase0_items(hc, half):
            items = []
            r0 = hc * HCK + half * 256

            def loads():
                S.do("gp", lambda e: e.dma_start(
                    out=p0_wg[:, :].rearrange("p (s d) -> p s d", s=2),
                    in_=Wgd[r0:r0 + 256, :].rearrange("(s p) d -> p s d", p=128)),
                    writes=[B["p0_wg"]])
                S.do("gp", lambda e: e.dma_start(
                    out=p0_wu[:, :].rearrange("p (s d) -> p s d", s=2),
                    in_=Wud[r0:r0 + 256, :].rearrange("(s p) d -> p s d", p=128)),
                    writes=[B["p0_wu"]])
                S.do("gp", lambda e: e.dma_start(
                    out=p0_wd[:, :].rearrange("p (c h) -> p c h", c=8),
                    in_=Wdd.rearrange("(c p) h -> p c h", p=128)[:, :, r0:r0 + 256]),
                    writes=[B["p0_wd"]])
            items.append(loads)

            jobs = []
            for s in range(2):
                for c in range(8):
                    jobs.append((B["p0_wg"], p0_wg[:, s * 1024 + c * 128:
                                                   s * 1024 + (c + 1) * 128],
                                 B["p0_gs"],
                                 p0_gs[:, c * 256 + s * 128:c * 256 + (s + 1) * 128]))
            for s in range(2):
                for c in range(8):
                    jobs.append((B["p0_wu"], p0_wu[:, s * 1024 + c * 128:
                                                   s * 1024 + (c + 1) * 128],
                                 B["p0_us"],
                                 p0_us[:, c * 256 + s * 128:c * 256 + (s + 1) * 128]))
            for c in range(8):
                for s in range(2):
                    jobs.append((B["p0_wd"], p0_wd[:, c * 256 + s * 128:
                                                   c * 256 + (s + 1) * 128],
                                 B["p0_ds"],
                                 p0_ds[:, s * 1024 + c * 128:s * 1024 + (c + 1) * 128]))
            for k0 in range(0, len(jobs), 4):
                batch = jobs[k0:k0 + 4]
                ps_t, ps_tb = (psA, B["psA"]) if (k0 // 4) % 2 == 0 \
                    else (psB, B["psB"])

                def run_batch(batch=batch, ps_t=ps_t, ps_tb=ps_tb):
                    srcs = sorted({id(b): b for b, _, _, _ in batch}.values(),
                                  key=lambda b: b.name)
                    S.group("pe",
                            [(lambda e, src=src, sl=sl, ps_t=ps_t: e.matmul(
                                ps_t[:, sl * 128:(sl + 1) * 128], src,
                                ident[:, :], start=True, stop=True))
                             for sl, (_, src, _, _) in enumerate(batch)],
                            reads=[B["ident"]] + list(srcs), writes=[ps_tb])
                    dsts = sorted({id(b): b for _, _, b, _ in batch}.values(),
                                  key=lambda b: b.name)
                    S.group("dve",
                            [(lambda e, dst=dst, sl=sl, ps_t=ps_t:
                              e.tensor_copy(dst, ps_t[:, sl * 128:(sl + 1) * 128]))
                             for sl, (_, _, _, dst) in enumerate(batch)],
                            reads=[ps_tb], writes=list(dsts))
                items.append(run_batch)

            def stores():
                S.do("sp", lambda e: e.dma_start(
                    out=wgTd[hc].rearrange("c p h -> p c h")[:, :, half * 256:
                                                             half * 256 + 256],
                    in_=p0_gs[:, :].rearrange("p (c h) -> p c h", c=8)),
                    reads=[B["p0_gs"]], writes=[B["wgTd"]])
                S.do("sp", lambda e: e.dma_start(
                    out=wuTd[hc].rearrange("c p h -> p c h")[:, :, half * 256:
                                                             half * 256 + 256],
                    in_=p0_us[:, :].rearrange("p (c h) -> p c h", c=8)),
                    reads=[B["p0_us"]], writes=[B["wuTd"]])
                S.do("sp", lambda e: e.dma_start(
                    out=wdTd[hc].rearrange("s p d -> p s d")[:, half * 2:half * 2 + 2, :],
                    in_=p0_ds[:, :].rearrange("p (s d) -> p s d", s=2)),
                    reads=[B["p0_ds"]], writes=[B["wdTd"]])
            items.append(stores)
            return items

        def layernorm_tile(src_ap, src_buf, rstd_col, nmr_col, pbuf,
                           scr, sbuf_, deq=False, skip_sum=False):
            S.do("act", lambda e: e.activation(
                xn1[:, :], src_ap, AF.Square,
                accum_out=stats[:, scr + 1:scr + 2]),
                reads=[src_buf], writes=[B["xn1"], sbuf_])
            if not skip_sum:
                S.do("dve", lambda e: e.reduce_sum(
                    stats[:, scr:scr + 1], src_ap, axis=AX.X),
                    reads=[src_buf], writes=[sbuf_])
            S.do("dve", lambda e: e.tensor_scalar(
                stats[:, scr + 2:scr + 4], stats[:, scr:scr + 2],
                1.0 / D, None, ALU.mult),
                reads=[sbuf_], writes=[sbuf_])
            S.do("dve", lambda e: e.tensor_tensor(
                stats[:, scr + 4:scr + 5], stats[:, scr + 2:scr + 3],
                stats[:, scr + 2:scr + 3], ALU.mult),
                reads=[sbuf_], writes=[sbuf_])
            S.do("dve", lambda e: e.tensor_sub(
                stats[:, scr + 5:scr + 6], stats[:, scr + 3:scr + 4],
                stats[:, scr + 4:scr + 5]),
                reads=[sbuf_], writes=[sbuf_])
            S.do("act", lambda e: e.activation(
                stats[:, scr + 6:scr + 7], stats[:, scr + 5:scr + 6],
                AF.Ln, bias=c_eps),
                reads=[sbuf_, B["consts"]], writes=[sbuf_])
            S.do("act", lambda e: e.activation(
                stats[:, rstd_col:rstd_col + 1], stats[:, scr + 6:scr + 7],
                AF.Exp, scale=c_mhalf),
                reads=[sbuf_, B["consts"]], writes=[pbuf])
            if deq:
                S.do("dve", lambda e: e.tensor_scalar(
                    stats[:, rstd_col:rstd_col + 1],
                    stats[:, rstd_col:rstd_col + 1], DEQ, None, ALU.mult),
                    reads=[pbuf], writes=[pbuf])
            S.do("dve", lambda e: e.scalar_tensor_tensor(
                stats[:, nmr_col:nmr_col + 1], stats[:, scr + 2:scr + 3],
                -1.0, stats[:, rstd_col:rstd_col + 1], ALU.mult, ALU.mult),
                reads=[sbuf_, pbuf], writes=[pbuf])

        def transpose_to_featmajor(tok_t, tok_buf, dst, dst_buf, i,
                                   use_cd=False):
            if use_cd:
                S.group("pe",
                        [(lambda e, c=c: e.matmul(
                            (psC if c < 4 else psD)
                            [:, (c % 4) * 128:(c % 4 + 1) * 128],
                            tok_t[:, c * 128:(c + 1) * 128], ident[:, :],
                            start=True, stop=True)) for c in range(8)],
                        reads=[B["ident"], tok_buf],
                        writes=[B["psC"], B["psD"]])
                for pst, pstb, c0 in ((psC, B["psC"], 0), (psD, B["psD"], 4)):
                    S.do("dve", lambda e, pst=pst, c0=c0: e.tensor_copy(
                        dst[:, :].rearrange("p (c s) -> p c s", c=8)
                        [:, c0:c0 + 4, i * 128:(i + 1) * 128],
                        pst[:, :].rearrange("p (c s) -> p c s", c=4)),
                        reads=[pstb], writes=[dst_buf])
                return
            pst, pstb = (psE2, B["psE2"]) if i % 2 == 0 else (psF2, B["psF2"])
            S.group("pe",
                    [(lambda e, c=c, pst=pst: e.matmul(
                        pst[:, c * 128:(c + 1) * 128],
                        tok_t[:, c * 128:(c + 1) * 128], ident[:, :],
                        start=True, stop=True)) for c in range(8)],
                    reads=[B["ident"], tok_buf], writes=[pstb])
            S.do("dve", lambda e, pst=pst: e.tensor_copy(
                dst[:, :].rearrange("p (c s) -> p c s", c=8)[:, :, i * 128:
                                                            (i + 1) * 128],
                pst[:, :].rearrange("p (c s) -> p c s", c=8)),
                reads=[pstb], writes=[dst_buf])

        def alias_guard(dsts, srcs):
            for d in dsts:
                for sb in srcs:
                    for eng, v in list(sb.w.items()) + list(sb.r.items()):
                        d.r[eng] = max(d.r.get(eng, 0), v)

        bg = []

        def bg_drain(n=1):
            for _ in range(min(n, len(bg))):
                bg.pop(0)()

        def ln1_stats(row0, i):
            sl = i % 2
            xs = B["xstg"][sl]
            S.do("sp", lambda e, i=i, sl=sl, row0=row0: e.dma_start(
                out=xstg[:, sl * 1024:(sl + 1) * 1024],
                in_=Xd[row0 + i * 128:row0 + (i + 1) * 128, :]),
                writes=[xs])
            layernorm_tile(xstg[:, sl * 1024:(sl + 1) * 1024], xs,
                           2 * i, 2 * i + 1, B["stP"][i],
                           64 + 8 * (i % 2), B["stS"][i % 2])

        def ln1_fill_tile(row0, i, redma, use_cd=False):
            sl = i % 2
            xs = B["xstg"][sl]
            if redma:
                S.do("sp", lambda e, i=i, sl=sl, row0=row0: e.dma_start(
                    out=xstg[:, sl * 1024:(sl + 1) * 1024],
                    in_=Xd[row0 + i * 128:row0 + (i + 1) * 128, :]),
                    writes=[xs])
            tk = i % 2
            S.do("act", lambda e, i=i, sl=sl, tk=tk: e.activation(
                xtoks[tk][:, :], xstg[:, sl * 1024:(sl + 1) * 1024],
                AF.Identity,
                bias=stats[:, 2 * i + 1:2 * i + 2],
                scale=stats[:, 2 * i:2 * i + 1]),
                reads=[xs, B["stP"][i]], writes=[B["xtok"][tk]])
            transpose_to_featmajor(xtoks[tk], B["xtok"][tk], xnf,
                                   B["xnf"], i, use_cd=use_cd)

        kcell = [0]
        fill_done = [False]

        def sequence(s):
            row0 = s * SEQ

            wT_bufs = [B["wgT"][0], B["wgT"][1], B["wuT"][0], B["wuT"][1],
                       B["wdT"][0], B["wdT"][1]]
            pair_bufs = [B["qp"][0], B["qp"][1], B["kp"][0], B["kp"][1],
                         B["vp"][0], B["vp"][1]]
            pairw_bufs = [B["wqp"][0], B["wqp"][1], B["wkp"][0], B["wkp"][1],
                          B["wvp"][0], B["wvp"][1]]
            if s > 0:
                alias_guard([B["atok"]], B["x1t"][0:8])
            alias_guard(pair_bufs, wT_bufs)
            alias_guard(pairw_bufs, [B["agelu"]])
            for sl in range(2):
                S.do("dve", lambda e, sl=sl: e.memset(vp[sl][:, :], 1.0),
                     writes=[B["vp"][sl]])

            if s == 0:
                def wpost_load():
                    S.do("gp", lambda e: e.dma_start(
                        out=wpost[:, :].rearrange("p (c d) -> p c d", c=8),
                        in_=Wpd.rearrange("(c p) d -> p c d", p=128)),
                        writes=[B["wpost"]])
                bg.append(wpost_load)
                for hc in range(NCK):
                    bg.extend(phase0_items(hc, 0))
                    bg.extend(phase0_items(hc, 1))
                for i in range(NT):
                    ln1_stats(row0, i)
                    ln1_fill_tile(row0, i, redma=False)
                    bg_drain(2)
            elif not fill_done[0]:
                for i in range(NT):
                    ln1_fill_tile(row0, i, redma=True)
            else:
                fill_done[0] = False

            def qkv_items(p):
                sl = p % 2
                items = []

                def loads(p=p, sl=sl):
                    for wd_ap, dst, dbuf in ((Wqd, wqp[sl], B["wqp"][sl]),
                                             (Wkd, wkp[sl], B["wkp"][sl]),
                                             (Wvd, wvp[sl], B["wvp"][sl])):
                        S.do("gp", lambda e, wd_ap=wd_ap, dst=dst, p=p:
                             e.dma_start(
                                 out=dst[:, :].rearrange("p (c n) -> p c n",
                                                         c=8),
                                 in_=wd_ap.rearrange("(c p) n -> p c n", p=128)
                                 [:, :, p * 128:(p + 1) * 128]),
                             writes=[dbuf])
                items.append(loads)

                for wi in range(2):
                    wsl_t = (wqp, wkp)[wi][sl]
                    wsl_b = (B["wqp"], B["wkp"])[wi][sl]
                    dstp = (qp, kp)[wi][sl]
                    dstb = (B["qp"], B["kp"])[wi][sl]
                    for jb in range(4):
                        ps, psb = (psA, B["psA"]) if (wi * 4 + jb) % 2 == 0 \
                            else (psB, B["psB"])

                        def qk_item(ps=ps, psb=psb, jb=jb, wsl_t=wsl_t,
                                    wsl_b=wsl_b, dstp=dstp, dstb=dstb):
                            S.group("pe",
                                    [(lambda e, c=c, ps=ps, jb=jb,
                                      wsl_t=wsl_t: e.matmul(
                                          ps[:, :],
                                          wsl_t[:, c * 128:(c + 1) * 128],
                                          xnf[:, c * SEQ + jb * 512:
                                              c * SEQ + (jb + 1) * 512],
                                          start=(c == 0), stop=(c == 7)))
                                     for c in range(8)],
                                    reads=[wsl_b, B["xnf"]], writes=[psb])
                            S.do("dve", lambda e, ps=ps, jb=jb, dstp=dstp:
                                 e.tensor_copy(
                                     dstp[:, jb * 512:(jb + 1) * 512],
                                     ps[:, :]),
                                 reads=[psb], writes=[dstb])
                        items.append(qk_item)

                for t in range(NT):
                    ps, psb = (psA, B["psA"]) if t % 2 == 0 else \
                        (psB, B["psB"])

                    def v_item(ps=ps, psb=psb, t=t, sl=sl):
                        S.group("pe",
                                [(lambda e, c=c, t=t, ps=ps, sl=sl: e.matmul(
                                    ps[:, 0:128],
                                    xnf[:, c * SEQ + t * 128:
                                        c * SEQ + (t + 1) * 128],
                                    wvp[sl][:, c * 128:(c + 1) * 128],
                                    start=(c == 0), stop=(c == 7)))
                                 for c in range(8)],
                                reads=[B["wvp"][sl], B["xnf"]], writes=[psb])
                        S.do("dve", lambda e, ps=ps, t=t, sl=sl: e.tensor_copy(
                            vp[sl][:, t * 130:(t + 1) * 130]
                            .rearrange("p (h v) -> p h v", v=65)[:, :, 0:64],
                            ps[:, 0:128].rearrange("p (h v) -> p h v", v=64)),
                            reads=[psb], writes=[B["vp"][sl]])
                    items.append(v_item)
                return items

            def emit_Sp(p, j, t, k):
                sl = p % 2
                sb, sbb = (psE2, B["psE2"]) if t % 2 == 0 else \
                    (psF2, B["psF2"])
                fns = []
                for half in range(2):
                    ro = half * 64
                    fns.append(lambda e, ro=ro, t=t, sb=sb, j=j, sl=sl:
                               e.matmul(
                                   sb[:, ro * 8:ro * 8 + 512],
                                   kp[sl][ro:ro + 64, t * 128:(t + 1) * 128],
                                   qp[sl][ro:ro + 64, j * 512:(j + 1) * 512],
                                   start=True, stop=True,
                                   tile_position=(ro, 0)))
                S.group("pe", fns, reads=[B["kp"][sl], B["qp"][sl]],
                        writes=[sbb])
                eten, eoff = e2sl[k % 3]
                S.do("act", lambda e, sb=sb, eten=eten, eoff=eoff:
                     e.activation(
                         eten[:, eoff:eoff + 1024], sb[:, :], AF.Exp,
                         bias=c_expsh, scale=c_expsc),
                     reads=[sbb, B["consts"]], writes=[B["E2"][k % 3]])

            def emit_AVp(p, j, t, k):
                sl = p % 2
                eten, eoff = e2sl[k % 3]
                esb = B["E2"][k % 3]
                for half in range(2):
                    at, atb = (psC, B["psC"]) if half == 0 else \
                        (psD, B["psD"])
                    S.do("pe", lambda e, t=t, half=half, at=at, eten=eten,
                         eoff=eoff, sl=sl: e.matmul(
                             at[0:65, :],
                             vp[sl][:, t * 130 + 65 * half:
                                    t * 130 + 65 * half + 65],
                             eten[:, eoff + half * 512:
                                  eoff + (half + 1) * 512],
                             start=(t == 0), stop=(t == NT - 1),
                             skip_group_check=True),
                         reads=[B["vp"][sl], esb], writes=[atb])

            def emit_epi_a(p, j):
                for half in range(2):
                    at, atb = (psC, B["psC"]) if half == 0 else \
                        (psD, B["psD"])
                    S.do("dve", lambda e, at=at, half=half: e.tensor_copy(
                        ae[half][0:65, :], at[0:65, :]),
                        reads=[atb], writes=[B["ae"][half]])

            def emit_epi_b(p, j):
                for half in range(2):
                    hl = 2 * p + half
                    ps_t, ps_tb = (psA, B["psA"]) if half == 0 else \
                        (psB, B["psB"])
                    ae_s, ae_b = ae[half], B["ae"][half]
                    rzs, rzb = rztmp[half], B["rzt"][half]
                    S.group("pe",
                            [(lambda e, qt=qt, ae_s=ae_s, ps_t=ps_t:
                              e.matmul(
                                  ps_t[:, qt * 65:(qt + 1) * 65],
                                  ae_s[0:65, qt * 128:(qt + 1) * 128],
                                  ident[0:65, 0:65],
                                  start=True, stop=True))
                             for qt in range(4)],
                            reads=[ae_b, B["ident"]], writes=[ps_tb])
                    S.do("dve", lambda e, ps_t=ps_t, rzs=rzs:
                         e.reciprocal(
                             rzs[:, 0:4].rearrange("p (i o) -> p i o", o=1),
                             ps_t[:, 0:260].rearrange("p (i o) -> p i o",
                                                      o=65)[:, :, 64:65]),
                         reads=[ps_tb], writes=[rzb])
                    for qt in range(4):
                        i = j * 4 + qt
                        S.do("dve", lambda e, qt=qt, i=i, hl=hl,
                             ps_t=ps_t, rzs=rzs: e.tensor_scalar(
                                 atok[:, i * 1024 + hl * 64:
                                      i * 1024 + (hl + 1) * 64],
                                 ps_t[:, qt * 65:qt * 65 + 64],
                                 rzs[:, qt:qt + 1], None, ALU.mult),
                             reads=[ps_tb, rzb], writes=[B["atok"]])

            for it in qkv_items(0):
                it()

            seqk = [(p, j, t) for p in range(NPAIR) for j in range(4)
                    for t in range(NT)]
            emit_Sp(*seqk[0], kcell[0])
            emit_Sp(*seqk[1], kcell[0] + 1)
            pend = []
            for k, (p, j, t) in enumerate(seqk):
                if j == 0 and t == 0 and p + 1 < NPAIR:
                    bg[:0] = qkv_items(p + 1)
                if k + 2 < len(seqk):
                    emit_Sp(*seqk[k + 2], kcell[0] + k + 2)
                while pend and pend[0][0] <= k:
                    pend.pop(0)[1]()
                bg_drain(1)
                emit_AVp(p, j, t, kcell[0] + k)
                if t == NT - 1:
                    pend.append((k + 1,
                                 lambda pp=p, pj=j: emit_epi_a(pp, pj)))
                    pend.append((k + 2,
                                 lambda pp=p, pj=j: emit_epi_b(pp, pj)))
            while pend:
                pend.pop(0)[1]()
            kcell[0] += len(seqk)

            alias_guard(B["x1t"][8:14],
                        [B["p0_wg"], B["p0_wu"], B["p0_wd"],
                         B["p0_gs"], B["p0_us"], B["p0_ds"]])
            alias_guard([B["agelu"]], pairw_bufs)
            ln2q = []
            for half in (1, 0):
                alias_guard([B["gt"][0], B["gt"][1]], [B["xn1"]])
                for it in range(8):
                    i = half * 8 + it
                    sl = it % 2
                    gts, gtb = gt[sl], B["gt"][sl]
                    bank0, bank0b, bank1, bank1b = \
                        (psA, B["psA"], psB, B["psB"]) if it % 2 == 0 else \
                        (psC, B["psC"], psD, B["psD"])
                    S.do("act", lambda e, gts=gts, i=i: e.activation(
                        gts[:, :], atok[:, i * 1024:(i + 1) * 1024], AF.Gelu),
                        reads=[B["atok"]], writes=[gtb])
                    S.group("pe",
                            [(lambda e, b=b, gts=gts, bank0=bank0,
                              bank1=bank1: e.matmul(
                                  (bank0 if b < 4 else bank1)
                                  [:, (b % 4) * 128:(b % 4 + 1) * 128],
                                  gts[:, b * 128:(b + 1) * 128], ident[:, :],
                                  start=True, stop=True)) for b in range(8)],
                            reads=[gtb, B["ident"]],
                            writes=[bank0b, bank1b])
                    for ps_t, ps_tb, c0 in ((bank0, bank0b, 0),
                                            (bank1, bank1b, 4)):
                        S.do("dve", lambda e, ps_t=ps_t, c0=c0, it=it:
                             e.tensor_copy(
                                 agelu[:, :].rearrange("p (c q) -> p c q", c=8)
                                 [:, c0:c0 + 4, it * 128:(it + 1) * 128],
                                 ps_t[:, :].rearrange("p (c q) -> p c q", c=4)),
                             reads=[ps_tb], writes=[B["agelu"]])
                if half == 0:
                    alias_guard(B["x1t"][0:8], [B["atok"]])
                alias_guard([B["xn1"]], [B["gt"][0], B["gt"][1]])
                for it in range(8):
                    i = half * 8 + it
                    sl = i % 2
                    xs = B["xstg"][sl]
                    S.do("sp", lambda e, i=i, sl=sl: e.dma_start(
                        out=xstg[:, sl * 1024:(sl + 1) * 1024],
                        in_=Xd[row0 + i * 128:row0 + (i + 1) * 128, :]),
                        writes=[xs])
                    S.do("dve", lambda e, sl=sl, i=i: e.scalar_tensor_tensor(
                        x1[:, i * 1024:(i + 1) * 1024],
                        xstg[:, sl * 1024:(sl + 1) * 1024],
                        stats[:, 2 * i:2 * i + 1],
                        xstg[:, sl * 1024:(sl + 1) * 1024],
                        ALU.mult, ALU.add),
                        reads=[xs, B["stP"][i]], writes=[B["x1t"][i]])
                    ps, psb = (psE2, B["psE2"]) if it % 2 == 0 else (psF2, B["psF2"])
                    fns = []
                    for nb in range(2):
                        for c in range(8):
                            fns.append(lambda e, c=c, nb=nb, it=it, ps=ps:
                                       e.matmul(
                                           ps[:, nb * 512:(nb + 1) * 512],
                                           agelu[:, c * 1024 + it * 128:
                                                 c * 1024 + (it + 1) * 128],
                                           wpost[:, c * D + nb * 512:
                                                 c * D + (nb + 1) * 512],
                                           start=(c == 0), stop=(c == 7)))
                    S.group("pe", fns, reads=[B["agelu"], B["wpost"]],
                            writes=[psb])
                    scr2 = 80 + 8 * (i % 2)
                    S.do("dve", lambda e, i=i, ps=ps, scr2=scr2:
                         e.scalar_tensor_tensor(
                             x1[:, i * 1024:(i + 1) * 1024],
                             x1[:, i * 1024:(i + 1) * 1024],
                             stats[:, 2 * i + 1:2 * i + 2], ps[:, :],
                             ALU.add, ALU.add,
                             accum_out=stats[:, scr2:scr2 + 1]),
                         reads=[psb, B["stP"][i], B["x1t"][i]],
                         writes=[B["x1t"][i], B["stS"][2 + i % 2]])
                    while len(ln2q) >= 2:
                        ln2q.pop(0)()
                    sl4 = i % 4
                    layernorm_tile(x1[:, i * 1024:(i + 1) * 1024],
                                   B["x1t"][i], 32 + 2 * sl4, 33 + 2 * sl4,
                                   B["stP2"][sl4], 80 + 8 * (i % 2),
                                   B["stS"][2 + i % 2], deq=True,
                                   skip_sum=True)

                    def ln2_tail(i=i, sl4=sl4):
                        tk = i % 2
                        S.do("act", lambda e, i=i, sl4=sl4, tk=tk:
                             e.activation(
                                 xtoks[tk][:, :],
                                 x1[:, i * 1024:(i + 1) * 1024],
                                 AF.Identity,
                                 bias=stats[:, 33 + 2 * sl4:34 + 2 * sl4],
                                 scale=stats[:, 32 + 2 * sl4:33 + 2 * sl4]),
                             reads=[B["x1t"][i], B["stP2"][sl4]],
                             writes=[B["xtok"][tk]])
                        transpose_to_featmajor(xtoks[tk], B["xtok"][tk],
                                               xnf, B["xnf"], i,
                                               use_cd=True)
                    ln2q.append(ln2_tail)
            while ln2q:
                ln2q.pop(0)()

            if s + 1 < NSEQ:
                bg.extend([lambda i=i: ln1_stats((s + 1) * SEQ, i)
                           for i in range(NT)])
            alias_guard(wT_bufs, pair_bufs)
            for hc in range(NCK):
                ws = hc % 2
                for src_d, dst_sb, dbuf, nchunks in (
                        (wgTd, wgT[ws], B["wgT"][ws], 8),
                        (wuTd, wuT[ws], B["wuT"][ws], 8),
                        (wdTd, wdT[ws], B["wdT"][ws], 4)):
                    S.do("gp", lambda e, src_d=src_d, dst_sb=dst_sb, hc=hc,
                         nchunks=nchunks: e.dma_start(
                             out=dst_sb[:, :].rearrange("p (c h) -> p c h",
                                                        c=nchunks),
                             in_=src_d[hc].rearrange("c p h -> p c h")),
                         reads=[B["wgTd" if src_d is wgTd else
                                  "wuTd" if src_d is wuTd else "wdTd"]],
                         writes=[dbuf])
                for j in range(4):
                    bg_drain(1)
                    for hsub in range(4):
                        psg, psgb = (psA, B["psA"]) if hsub % 2 == 0 else (psB, B["psB"])
                        psu, psub = (psC, B["psC"]) if hsub % 2 == 0 else (psD, B["psD"])
                        S.group("pe",
                                [(lambda e, c=c, hsub=hsub, j=j, ws=ws, psg=psg:
                                  e.matmul(
                                      psg[:, :],
                                      wgT[ws][:, c * 512 + hsub * 128:
                                              c * 512 + (hsub + 1) * 128],
                                      xnf[:, c * SEQ + j * 512:
                                          c * SEQ + (j + 1) * 512],
                                      start=(c == 0), stop=(c == 7)))
                                 for c in range(8)],
                                reads=[B["wgT"][ws], B["xnf"]], writes=[psgb])
                        S.group("pe",
                                [(lambda e, c=c, hsub=hsub, j=j, ws=ws, psu=psu:
                                  e.matmul(
                                      psu[:, :],
                                      wuT[ws][:, c * 512 + hsub * 128:
                                              c * 512 + (hsub + 1) * 128],
                                      xnf[:, c * SEQ + j * 512:
                                          c * SEQ + (j + 1) * 512],
                                      start=(c == 0), stop=(c == 7)))
                                 for c in range(8)],
                                reads=[B["wuT"][ws], B["xnf"]], writes=[psub])
                        S.do("act", lambda e, psg=psg: e.activation(
                            sg[:, :], psg[:, :], AF.Silu),
                            reads=[psgb], writes=[B["sg"]])
                        S.do("dve", lambda e, psu=psu, hsub=hsub:
                             e.scalar_tensor_tensor(
                                 hh[:, hsub * 512:(hsub + 1) * 512],
                                 psu[:, :], DEQ, sg[:, :], ALU.mult, ALU.mult),
                             reads=[psub, B["sg"]], writes=[B["hh"][hsub]])
                    if hc == NCK - 1 and s + 1 < NSEQ and j >= 1:
                        for fi in range(4 * (j - 1), 4 * j):
                            ln1_fill_tile((s + 1) * SEQ, fi, redma=True,
                                          use_cd=True)
                    for it in range(4):
                        i = j * 4 + it
                        psd_, psdb = (psE2, B["psE2"]) if it % 2 == 0 \
                            else (psF2, B["psF2"])
                        for hsub in range(4):
                            S.group("pe",
                                    [(lambda e, hsub=hsub, nb=nb, it=it,
                                      ws=ws, psd_=psd_: e.matmul(
                                          psd_[:, nb * 512:(nb + 1) * 512],
                                          hh[:, hsub * 512 + it * 128:
                                             hsub * 512 + (it + 1) * 128],
                                          wdT[ws][:, hsub * D + nb * 512:
                                                  hsub * D + (nb + 1) * 512],
                                          start=(hsub == 0), stop=(hsub == 3)))
                                     for nb in range(2)],
                                    reads=[B["hh"][hsub], B["wdT"][ws]],
                                    writes=[psdb])
                        S.do("dve", lambda e, i=i, psd_=psd_: e.tensor_add(
                            x1[:, i * 1024:(i + 1) * 1024],
                            x1[:, i * 1024:(i + 1) * 1024], psd_[:, :]),
                            reads=[psdb, B["x1t"][i]], writes=[B["x1t"][i]])
                        if hc == NCK - 1:
                            S.do("act", lambda e, i=i: e.dma_start(
                                out=Yd[row0 + i * 128:row0 + (i + 1) * 128, :],
                                in_=x1[:, i * 1024:(i + 1) * 1024]),
                                reads=[B["x1t"][i]], inc=16)

            if s + 1 < NSEQ:
                for fi in range(12, NT):
                    ln1_fill_tile((s + 1) * SEQ, fi, redma=True,
                                  use_cd=True)
                fill_done[0] = True

        for s in range(NSEQ):
            sequence(s)
        assert not bg, "background queue not fully drained"

        for q in ("sp", "gp", "act"):
            v = S.cnt[q]
            S.plan[q].append(
                ([], lambda e, q=q, v=v: e.wait_ge(sems[q], v), 0))

        def emit(ename):
            plan = S.plan[ename]

            def emitter(e):
                for wl, fn, inc in plan:
                    for (p, v) in wl:
                        e.wait_ge(sems[p], v)
                    ins = fn(e)
                    if inc:
                        ins.then_inc(sems[ename], inc)
            return emitter

        block.sync(emit("sp"))
        block.gpsimd(emit("gp"))
        block.tensor(emit("pe"))
        block.scalar(emit("act"))
        block.vector(emit("dve"))

    return nc


_NC_CACHE = {}


def kernel(L0_attn_X, L0_attn_Wq, L0_attn_Wk, L0_attn_Wv, L0_attn_Wpost,
           L0_ffn_w_gate, L0_ffn_w_up, L0_ffn_w_down):
    X = np.ascontiguousarray(np.asarray(L0_attn_X), dtype=np.float32)
    Wq = np.ascontiguousarray(np.asarray(L0_attn_Wq), dtype=np.float32)
    Wk = np.ascontiguousarray(np.asarray(L0_attn_Wk), dtype=np.float32)
    Wv = np.ascontiguousarray(np.asarray(L0_attn_Wv), dtype=np.float32)
    Wp = np.ascontiguousarray(np.asarray(L0_attn_Wpost), dtype=np.float32)
    Wg = np.ascontiguousarray(np.asarray(L0_ffn_w_gate), dtype=np.int32)
    Wu = np.ascontiguousarray(np.asarray(L0_ffn_w_up), dtype=np.int32)
    Wd = np.ascontiguousarray(np.asarray(L0_ffn_w_down), dtype=np.int32)

    if "nc" not in _NC_CACHE:
        _NC_CACHE["nc"] = build_program()
    nc = _NC_CACHE["nc"]

    in_maps = []
    for c in range(NCORES):
        xs = X[c * NSEQ:(c + 1) * NSEQ].reshape(NSEQ * SEQ, D)
        in_maps.append({"X": np.ascontiguousarray(xs), "Wq": Wq, "Wk": Wk,
                        "Wv": Wv, "Wp": Wp, "Wg": Wg, "Wu": Wu, "Wd": Wd})
    res = run_bass_kernel_spmd(nc, in_maps, core_ids=list(range(NCORES)))
    out = np.concatenate(
        [res.results[c]["Y"].reshape(NSEQ, SEQ, D) for c in range(NCORES)],
        axis=0)
    return out


# version 35
# speedup vs baseline: 1.0541x; 1.0230x over previous
import contextlib

import numpy as np
import ml_dtypes

import concourse.bass as bass
import concourse.mybir as mybir
from concourse.bass_utils import run_bass_kernel_spmd

F32 = mybir.dt.float32
BF16 = mybir.dt.bfloat16
I32 = mybir.dt.int32
AF = mybir.ActivationFunctionType
ALU = mybir.AluOpType
AX = mybir.AxisListType
BF = ml_dtypes.bfloat16

NCORES = 8
NSEQ = 2
SEQ = 2048
D = 1024
NT = SEQ // 128
HEADS = 16
NPAIR = 8
HFF = 3072
HCK = 512
NCK = HFF // HCK
EXP_SCALE = 0.125
EXP_SHIFT = -3.0
DEQ = 1.0 / 127.0

OFF_IDENT = 0
OFF_CONST = 512
OFF_STATS = 640
OFF_AE = 1024
OFF_E2C = 3072
OFF_RZT = 256
OFF_XTOK2 = 5120
OFF_XSTG = 7168
OFF_XN1 = 15360
OFF_GT = 15360
OFF_SG = 19456
OFF_HH = 21504
OFF_X1 = 25600
OFF_ATOK = 25600
OFF_PH0 = 58368
OFF_XNF = 91136
OFF_S2 = 123904
OFF_QP = 123904
OFF_KP = 132096
OFF_VP = 140288
OFF_WT0 = 123904
OFF_WT1 = 148480
OFF_WPOST = 173312
OFF_WP = 189696
OFF_AGELU = 189696
OFF_XNTOK = 206080
OFF_E2 = 208128


class Buf:

    __slots__ = ("name", "w", "r")

    def __init__(self, name):
        self.name = name
        self.w = {}
        self.r = {}


class Sch:
    ENGS = ("pe", "act", "dve", "sp", "gp")

    def __init__(self):
        self.plan = {e: [] for e in self.ENGS}
        self.cnt = {e: 0 for e in self.ENGS}
        self.waited = {e: {p: 0 for p in self.ENGS} for e in self.ENGS}

    def _op(self, eng, fn, waits, inc):
        wl = []
        if eng in ("sp", "gp") and self.cnt[eng] > 0:
            wl.append((eng, self.cnt[eng]))
        for (p, v) in waits:
            if p == eng and eng in ("pe", "sp", "gp"):
                continue
            if v > self.waited[eng][p]:
                self.waited[eng][p] = v
                wl.append((p, v))
        wl.sort()
        self.cnt[eng] += inc
        self.plan[eng].append((wl, fn, inc))
        return (eng, self.cnt[eng])

    def do(self, eng, fn, reads=(), writes=(), extra=(), inc=None):
        return self.group(eng, [fn], reads=reads, writes=writes, extra=extra,
                          inc=inc)

    def group(self, eng, fns, reads=(), writes=(), extra=(), inc=None):
        if inc is None:
            inc = 16 if eng in ("sp", "gp") else 1
        waits = list(extra)
        for b in reads:
            waits += list(b.w.items())
        for b in writes:
            waits += list(b.w.items())
            waits += list(b.r.items())
        tok = None
        for k, fn in enumerate(fns):
            tok = self._op(eng, fn, waits if k == 0 else (),
                           inc if k == len(fns) - 1 else 0)
        _, v = tok
        for b in reads:
            b.r[eng] = max(b.r.get(eng, 0), v)
        for b in writes:
            b.w[eng] = max(b.w.get(eng, 0), v)
        return tok


def build_program():
    nc = bass.Bass("TRN2", target_bir_lowering=False, debug=False)

    Xd = nc.dram_tensor("X", [NSEQ * SEQ, D], F32, kind="ExternalInput").ap()
    Wqd = nc.dram_tensor("Wq", [D, D], F32, kind="ExternalInput").ap()
    Wkd = nc.dram_tensor("Wk", [D, D], F32, kind="ExternalInput").ap()
    Wvd = nc.dram_tensor("Wv", [D, D], F32, kind="ExternalInput").ap()
    Wpd = nc.dram_tensor("Wp", [D, D], F32, kind="ExternalInput").ap()
    Wgd = nc.dram_tensor("Wg", [HFF, D], I32, kind="ExternalInput").ap()
    Wud = nc.dram_tensor("Wu", [HFF, D], I32, kind="ExternalInput").ap()
    Wdd = nc.dram_tensor("Wd", [D, HFF], I32, kind="ExternalInput").ap()
    Yd = nc.dram_tensor("Y", [NSEQ * SEQ, D], F32, kind="ExternalOutput").ap()

    wgTd = nc.dram_tensor("wgT_s", [NCK, 8, 128, HCK], BF16, kind="Internal").ap()
    wuTd = nc.dram_tensor("wuT_s", [NCK, 8, 128, HCK], BF16, kind="Internal").ap()
    wdTd = nc.dram_tensor("wdT_s", [NCK, 4, 128, D], BF16, kind="Internal").ap()
    identd = nc.inline_tensor(np.eye(128, dtype=BF), "identd").ap()

    _dummy = nc.alloc_sbuf_tensor("map_base", [128, 64], mybir.dt.uint8)
    _base = nc.lookup_mloc(_dummy).addr + 64

    A = lambda name, shape, dt, off: nc.alloc_sbuf_tensor_at(
        name, shape, dt, offset=_base + off)

    ident = A("ident", [128, 128], BF16, OFF_IDENT)
    consts = A("consts", [128, 16], F32, OFF_CONST)
    stats = A("stats", [128, 96], F32, OFF_STATS)
    ae = [A("ae0", [128, 512], BF16, OFF_AE),
          A("ae1", [128, 512], BF16, OFF_AE + 1024)]
    e2c = A("e2c", [128, 1024], BF16, OFF_E2C)
    rztmp = [A("rzt0", [128, 8], F32, OFF_RZT),
             A("rzt1", [128, 8], F32, OFF_RZT + 32)]
    xtok2 = A("xtok2", [128, 1024], BF16, OFF_XTOK2)
    xstg = A("xstg", [128, 2048], F32, OFF_XSTG)
    xn1 = A("xn1", [128, 1024], F32, OFF_XN1)
    gt = [A("gt0", [128, 1024], BF16, OFF_GT),
          A("gt1", [128, 1024], BF16, OFF_GT + 2048)]
    sg = A("sg", [128, 512], F32, OFF_SG)
    hh = A("hh", [128, 2048], BF16, OFF_HH)
    x1 = A("x1", [128, 16384], F32, OFF_X1)
    atok = A("atok", [128, 16384], BF16, OFF_ATOK)
    xnf = A("xnf", [128, 16384], BF16, OFF_XNF)
    qp = [A("qp0", [128, 2048], BF16, OFF_QP),
          A("qp1", [128, 2048], BF16, OFF_QP + 4096)]
    kp = [A("kp0", [128, 2048], BF16, OFF_KP),
          A("kp1", [128, 2048], BF16, OFF_KP + 4096)]
    vp = [A("vp0", [128, NT * 130], BF16, OFF_VP),
          A("vp1", [128, NT * 130], BF16, OFF_VP + 4160)]
    wpost = A("wpost", [128, 8 * D], BF16, OFF_WPOST)
    wqp = [A("wqp0", [128, 1024], BF16, OFF_WP),
           A("wqp1", [128, 1024], BF16, OFF_WP + 2048)]
    wkp = [A("wkp0", [128, 1024], BF16, OFF_WP + 4096),
           A("wkp1", [128, 1024], BF16, OFF_WP + 6144)]
    wvp = [A("wvp0", [128, 1024], BF16, OFF_WP + 8192),
           A("wvp1", [128, 1024], BF16, OFF_WP + 10240)]
    agelu = A("agelu", [128, 8192], BF16, OFF_AGELU)
    xtok = A("xtok", [128, 1024], BF16, OFF_XNTOK)
    xtoks = [xtok, xtok2]
    e2 = A("e2", [128, 2048], BF16, OFF_E2)
    p0_wg = A("p0_wg", [128, 2048], BF16, OFF_PH0)
    p0_wu = A("p0_wu", [128, 2048], BF16, OFF_PH0 + 4096)
    p0_wd = A("p0_wd", [128, 2048], BF16, OFF_PH0 + 8192)
    p0_gs = A("p0_gs", [128, 8 * 256], BF16, OFF_PH0 + 12288)
    p0_us = A("p0_us", [128, 8 * 256], BF16, OFF_PH0 + 16384)
    p0_ds = A("p0_ds", [128, 2 * 1024], BF16, OFF_PH0 + 20480)
    wgT = [A("wgT0", [128, 8 * HCK], BF16, OFF_WT0),
           A("wgT1", [128, 8 * HCK], BF16, OFF_WT1)]
    wuT = [A("wuT0", [128, 8 * HCK], BF16, OFF_WT0 + 8192),
           A("wuT1", [128, 8 * HCK], BF16, OFF_WT1 + 8192)]
    wdT = [A("wdT0", [128, 4 * D], BF16, OFF_WT0 + 16384),
           A("wdT1", [128, 4 * D], BF16, OFF_WT1 + 16384)]

    with contextlib.ExitStack() as stk:
        psA = stk.enter_context(nc.psum_tensor("psA", [128, 512], F32))
        psB = stk.enter_context(nc.psum_tensor("psB", [128, 512], F32))
        psC = stk.enter_context(nc.psum_tensor("psC", [128, 512], F32))
        psD = stk.enter_context(nc.psum_tensor("psD", [128, 512], F32))
        psE2 = stk.enter_context(nc.psum_tensor("psE2", [128, 1024], F32))
        psF2 = stk.enter_context(nc.psum_tensor("psF2", [128, 1024], F32))
        sems = {k: stk.enter_context(nc.semaphore("sem_" + k))
                for k in Sch.ENGS}
        block = stk.enter_context(nc.Block())

        S = Sch()
        B = {}
        for nm in ("ident", "consts", "stats",
                   "xn1", "sg", "atok", "xnf", "wpost",
                   "agelu", "psA", "psB", "psC", "psD", "psE2",
                   "psF2", "p0_wg", "p0_wu", "p0_wd", "p0_gs", "p0_us",
                   "p0_ds", "wgTd", "wuTd", "wdTd"):
            B[nm] = Buf(nm)
        B["ae"] = [Buf("ae0"), Buf("ae1")]
        B["rzt"] = [Buf("rzt0"), Buf("rzt1")]
        B["gt"] = [Buf("gt0"), Buf("gt1")]
        B["xtok"] = [Buf("xtok0"), Buf("xtok1")]
        B["atokT"] = [Buf("atokT%d" % i) for i in range(NT)]
        B["ageluT"] = [Buf("ageluT%d" % i) for i in range(8)]
        B["qp"] = [Buf("qp0"), Buf("qp1")]
        B["kp"] = [Buf("kp0"), Buf("kp1")]
        B["vp"] = [Buf("vp0"), Buf("vp1")]
        B["wqp"] = [Buf("wqp0"), Buf("wqp1")]
        B["wkp"] = [Buf("wkp0"), Buf("wkp1")]
        B["wvp"] = [Buf("wvp0"), Buf("wvp1")]
        B["stP"] = [Buf("stP%d" % i) for i in range(NT)]
        B["stP2"] = [Buf("stP2_%d" % i) for i in range(4)]
        B["stS"] = [Buf("stS%d" % i) for i in range(4)]
        B["E2"] = [Buf("E2a"), Buf("E2b"), Buf("E2c")]
        e2sl = [(e2, 0), (e2, 1024), (e2c, 0)]
        B["xstg"] = [Buf("xstg0"), Buf("xstg1")]
        B["hh"] = [Buf("hh%d" % k) for k in range(4)]
        B["x1t"] = [Buf("x1_%d" % i) for i in range(NT)]
        B["wgT"] = [Buf("wgT0"), Buf("wgT1")]
        B["wuT"] = [Buf("wuT0"), Buf("wuT1")]
        B["wdT"] = [Buf("wdT0"), Buf("wdT1")]

        c_expsc = consts[:, 0:1]
        c_expsh = consts[:, 1:2]
        c_eps = consts[:, 2:3]
        c_mhalf = consts[:, 3:4]

        S.do("sp", lambda e: e.dma_start(out=ident[:, :], in_=identd),
             writes=[B["ident"]])
        for col, val in ((0, EXP_SCALE), (1, EXP_SHIFT), (2, 1e-5), (3, -0.5)):
            S.do("dve",
                 lambda e, col=col, val=val: e.memset(consts[:, col:col + 1], val),
                 writes=[B["consts"]])

        def phase0_items(hc, half):
            items = []
            r0 = hc * HCK + half * 256

            def loads():
                S.do("gp", lambda e: e.dma_start(
                    out=p0_wg[:, :].rearrange("p (s d) -> p s d", s=2),
                    in_=Wgd[r0:r0 + 256, :].rearrange("(s p) d -> p s d", p=128)),
                    writes=[B["p0_wg"]])
                S.do("gp", lambda e: e.dma_start(
                    out=p0_wu[:, :].rearrange("p (s d) -> p s d", s=2),
                    in_=Wud[r0:r0 + 256, :].rearrange("(s p) d -> p s d", p=128)),
                    writes=[B["p0_wu"]])
                S.do("gp", lambda e: e.dma_start(
                    out=p0_wd[:, :].rearrange("p (c h) -> p c h", c=8),
                    in_=Wdd.rearrange("(c p) h -> p c h", p=128)[:, :, r0:r0 + 256]),
                    writes=[B["p0_wd"]])
            items.append(loads)

            jobs = []
            for s in range(2):
                for c in range(8):
                    jobs.append((B["p0_wg"], p0_wg[:, s * 1024 + c * 128:
                                                   s * 1024 + (c + 1) * 128],
                                 B["p0_gs"],
                                 p0_gs[:, c * 256 + s * 128:c * 256 + (s + 1) * 128]))
            for s in range(2):
                for c in range(8):
                    jobs.append((B["p0_wu"], p0_wu[:, s * 1024 + c * 128:
                                                   s * 1024 + (c + 1) * 128],
                                 B["p0_us"],
                                 p0_us[:, c * 256 + s * 128:c * 256 + (s + 1) * 128]))
            for c in range(8):
                for s in range(2):
                    jobs.append((B["p0_wd"], p0_wd[:, c * 256 + s * 128:
                                                   c * 256 + (s + 1) * 128],
                                 B["p0_ds"],
                                 p0_ds[:, s * 1024 + c * 128:s * 1024 + (c + 1) * 128]))
            for k0 in range(0, len(jobs), 4):
                batch = jobs[k0:k0 + 4]
                ps_t, ps_tb = (psA, B["psA"]) if (k0 // 4) % 2 == 0 \
                    else (psB, B["psB"])

                def run_batch(batch=batch, ps_t=ps_t, ps_tb=ps_tb):
                    srcs = sorted({id(b): b for b, _, _, _ in batch}.values(),
                                  key=lambda b: b.name)
                    S.group("pe",
                            [(lambda e, src=src, sl=sl, ps_t=ps_t: e.matmul(
                                ps_t[:, sl * 128:(sl + 1) * 128], src,
                                ident[:, :], start=True, stop=True))
                             for sl, (_, src, _, _) in enumerate(batch)],
                            reads=[B["ident"]] + list(srcs), writes=[ps_tb])
                    dsts = sorted({id(b): b for _, _, b, _ in batch}.values(),
                                  key=lambda b: b.name)
                    S.group("dve",
                            [(lambda e, dst=dst, sl=sl, ps_t=ps_t:
                              e.tensor_copy(dst, ps_t[:, sl * 128:(sl + 1) * 128]))
                             for sl, (_, _, _, dst) in enumerate(batch)],
                            reads=[ps_tb], writes=list(dsts))
                items.append(run_batch)

            def stores():
                S.do("sp", lambda e: e.dma_start(
                    out=wgTd[hc].rearrange("c p h -> p c h")[:, :, half * 256:
                                                             half * 256 + 256],
                    in_=p0_gs[:, :].rearrange("p (c h) -> p c h", c=8)),
                    reads=[B["p0_gs"]], writes=[B["wgTd"]])
                S.do("sp", lambda e: e.dma_start(
                    out=wuTd[hc].rearrange("c p h -> p c h")[:, :, half * 256:
                                                             half * 256 + 256],
                    in_=p0_us[:, :].rearrange("p (c h) -> p c h", c=8)),
                    reads=[B["p0_us"]], writes=[B["wuTd"]])
                S.do("sp", lambda e: e.dma_start(
                    out=wdTd[hc].rearrange("s p d -> p s d")[:, half * 2:half * 2 + 2, :],
                    in_=p0_ds[:, :].rearrange("p (s d) -> p s d", s=2)),
                    reads=[B["p0_ds"]], writes=[B["wdTd"]])
            items.append(stores)
            return items

        def layernorm_tile(src_ap, src_buf, rstd_col, nmr_col, pbuf,
                           scr, sbuf_, deq=False, skip_sum=False):
            S.do("act", lambda e: e.activation(
                xn1[:, :], src_ap, AF.Square,
                accum_out=stats[:, scr + 1:scr + 2]),
                reads=[src_buf], writes=[B["xn1"], sbuf_])
            if not skip_sum:
                S.do("dve", lambda e: e.reduce_sum(
                    stats[:, scr:scr + 1], src_ap, axis=AX.X),
                    reads=[src_buf], writes=[sbuf_])
            S.do("dve", lambda e: e.tensor_scalar(
                stats[:, scr + 2:scr + 4], stats[:, scr:scr + 2],
                1.0 / D, None, ALU.mult),
                reads=[sbuf_], writes=[sbuf_])
            S.do("dve", lambda e: e.tensor_tensor(
                stats[:, scr + 4:scr + 5], stats[:, scr + 2:scr + 3],
                stats[:, scr + 2:scr + 3], ALU.mult),
                reads=[sbuf_], writes=[sbuf_])
            S.do("dve", lambda e: e.tensor_sub(
                stats[:, scr + 5:scr + 6], stats[:, scr + 3:scr + 4],
                stats[:, scr + 4:scr + 5]),
                reads=[sbuf_], writes=[sbuf_])
            S.do("act", lambda e: e.activation(
                stats[:, scr + 6:scr + 7], stats[:, scr + 5:scr + 6],
                AF.Ln, bias=c_eps),
                reads=[sbuf_, B["consts"]], writes=[sbuf_])
            S.do("act", lambda e: e.activation(
                stats[:, rstd_col:rstd_col + 1], stats[:, scr + 6:scr + 7],
                AF.Exp, scale=c_mhalf),
                reads=[sbuf_, B["consts"]], writes=[pbuf])
            if deq:
                S.do("dve", lambda e: e.tensor_scalar(
                    stats[:, rstd_col:rstd_col + 1],
                    stats[:, rstd_col:rstd_col + 1], DEQ, None, ALU.mult),
                    reads=[pbuf], writes=[pbuf])
            S.do("dve", lambda e: e.scalar_tensor_tensor(
                stats[:, nmr_col:nmr_col + 1], stats[:, scr + 2:scr + 3],
                -1.0, stats[:, rstd_col:rstd_col + 1], ALU.mult, ALU.mult),
                reads=[sbuf_, pbuf], writes=[pbuf])

        def transpose_to_featmajor(tok_t, tok_buf, dst, dst_buf, i,
                                   use_cd=False):
            if use_cd:
                S.group("pe",
                        [(lambda e, c=c: e.matmul(
                            (psC if c < 4 else psD)
                            [:, (c % 4) * 128:(c % 4 + 1) * 128],
                            tok_t[:, c * 128:(c + 1) * 128], ident[:, :],
                            start=True, stop=True)) for c in range(8)],
                        reads=[B["ident"], tok_buf],
                        writes=[B["psC"], B["psD"]])
                for pst, pstb, c0 in ((psC, B["psC"], 0), (psD, B["psD"], 4)):
                    S.do("dve", lambda e, pst=pst, c0=c0: e.tensor_copy(
                        dst[:, :].rearrange("p (c s) -> p c s", c=8)
                        [:, c0:c0 + 4, i * 128:(i + 1) * 128],
                        pst[:, :].rearrange("p (c s) -> p c s", c=4)),
                        reads=[pstb], writes=[dst_buf])
                return
            pst, pstb = (psE2, B["psE2"]) if i % 2 == 0 else (psF2, B["psF2"])
            S.group("pe",
                    [(lambda e, c=c, pst=pst: e.matmul(
                        pst[:, c * 128:(c + 1) * 128],
                        tok_t[:, c * 128:(c + 1) * 128], ident[:, :],
                        start=True, stop=True)) for c in range(8)],
                    reads=[B["ident"], tok_buf], writes=[pstb])
            S.do("dve", lambda e, pst=pst: e.tensor_copy(
                dst[:, :].rearrange("p (c s) -> p c s", c=8)[:, :, i * 128:
                                                            (i + 1) * 128],
                pst[:, :].rearrange("p (c s) -> p c s", c=8)),
                reads=[pstb], writes=[dst_buf])

        def alias_guard(dsts, srcs):
            for d in dsts:
                for sb in srcs:
                    for eng, v in list(sb.w.items()) + list(sb.r.items()):
                        d.r[eng] = max(d.r.get(eng, 0), v)

        bg = []

        def bg_drain(n=1):
            for _ in range(min(n, len(bg))):
                bg.pop(0)()

        def ln1_stats(row0, i):
            sl = i % 2
            xs = B["xstg"][sl]
            S.do("sp", lambda e, i=i, sl=sl, row0=row0: e.dma_start(
                out=xstg[:, sl * 1024:(sl + 1) * 1024],
                in_=Xd[row0 + i * 128:row0 + (i + 1) * 128, :]),
                writes=[xs])
            layernorm_tile(xstg[:, sl * 1024:(sl + 1) * 1024], xs,
                           2 * i, 2 * i + 1, B["stP"][i],
                           64 + 8 * (i % 2), B["stS"][i % 2])

        def ln1_fill_tile(row0, i, redma, use_cd=False):
            sl = i % 2
            xs = B["xstg"][sl]
            if redma:
                S.do("sp", lambda e, i=i, sl=sl, row0=row0: e.dma_start(
                    out=xstg[:, sl * 1024:(sl + 1) * 1024],
                    in_=Xd[row0 + i * 128:row0 + (i + 1) * 128, :]),
                    writes=[xs])
            tk = i % 2
            S.do("act", lambda e, i=i, sl=sl, tk=tk: e.activation(
                xtoks[tk][:, :], xstg[:, sl * 1024:(sl + 1) * 1024],
                AF.Identity,
                bias=stats[:, 2 * i + 1:2 * i + 2],
                scale=stats[:, 2 * i:2 * i + 1]),
                reads=[xs, B["stP"][i]], writes=[B["xtok"][tk]])
            transpose_to_featmajor(xtoks[tk], B["xtok"][tk], xnf,
                                   B["xnf"], i, use_cd=use_cd)

        kcell = [0]
        fill_done = [False]

        def sequence(s):
            row0 = s * SEQ

            wT_bufs = [B["wgT"][0], B["wgT"][1], B["wuT"][0], B["wuT"][1],
                       B["wdT"][0], B["wdT"][1]]
            pair_bufs = [B["qp"][0], B["qp"][1], B["kp"][0], B["kp"][1],
                         B["vp"][0], B["vp"][1]]
            pairw_bufs = [B["wqp"][0], B["wqp"][1], B["wkp"][0], B["wkp"][1],
                          B["wvp"][0], B["wvp"][1]]
            if s > 0:
                alias_guard(B["atokT"], B["x1t"][0:8])
            alias_guard(pair_bufs, wT_bufs)
            alias_guard(pairw_bufs, B["ageluT"])
            for sl in range(2):
                S.do("dve", lambda e, sl=sl: e.memset(vp[sl][:, :], 1.0),
                     writes=[B["vp"][sl]])

            if s == 0:
                def wpost_load():
                    S.do("gp", lambda e: e.dma_start(
                        out=wpost[:, :].rearrange("p (c d) -> p c d", c=8),
                        in_=Wpd.rearrange("(c p) d -> p c d", p=128)),
                        writes=[B["wpost"]])
                bg.append(wpost_load)
                for hc in range(NCK):
                    bg.extend(phase0_items(hc, 0))
                    bg.extend(phase0_items(hc, 1))
                for i in range(NT):
                    ln1_stats(row0, i)
                    ln1_fill_tile(row0, i, redma=False)
                    bg_drain(2)
            elif not fill_done[0]:
                for i in range(NT):
                    ln1_fill_tile(row0, i, redma=True)
            else:
                fill_done[0] = False

            def qkv_items(p):
                sl = p % 2
                items = []

                def loads(p=p, sl=sl):
                    for wd_ap, dst, dbuf in ((Wqd, wqp[sl], B["wqp"][sl]),
                                             (Wkd, wkp[sl], B["wkp"][sl]),
                                             (Wvd, wvp[sl], B["wvp"][sl])):
                        S.do("gp", lambda e, wd_ap=wd_ap, dst=dst, p=p:
                             e.dma_start(
                                 out=dst[:, :].rearrange("p (c n) -> p c n",
                                                         c=8),
                                 in_=wd_ap.rearrange("(c p) n -> p c n", p=128)
                                 [:, :, p * 128:(p + 1) * 128]),
                             writes=[dbuf])
                items.append(loads)

                for wi in range(2):
                    wsl_t = (wqp, wkp)[wi][sl]
                    wsl_b = (B["wqp"], B["wkp"])[wi][sl]
                    dstp = (qp, kp)[wi][sl]
                    dstb = (B["qp"], B["kp"])[wi][sl]
                    for jb in range(4):
                        ps, psb = (psA, B["psA"]) if (wi * 4 + jb) % 2 == 0 \
                            else (psB, B["psB"])

                        def qk_item(ps=ps, psb=psb, jb=jb, wsl_t=wsl_t,
                                    wsl_b=wsl_b, dstp=dstp, dstb=dstb):
                            S.group("pe",
                                    [(lambda e, c=c, ps=ps, jb=jb,
                                      wsl_t=wsl_t: e.matmul(
                                          ps[:, :],
                                          wsl_t[:, c * 128:(c + 1) * 128],
                                          xnf[:, c * SEQ + jb * 512:
                                              c * SEQ + (jb + 1) * 512],
                                          start=(c == 0), stop=(c == 7)))
                                     for c in range(8)],
                                    reads=[wsl_b, B["xnf"]], writes=[psb])
                            S.do("dve", lambda e, ps=ps, jb=jb, dstp=dstp:
                                 e.tensor_copy(
                                     dstp[:, jb * 512:(jb + 1) * 512],
                                     ps[:, :]),
                                 reads=[psb], writes=[dstb])
                        items.append(qk_item)

                for t in range(NT):
                    ps, psb = (psA, B["psA"]) if t % 2 == 0 else \
                        (psB, B["psB"])

                    def v_item(ps=ps, psb=psb, t=t, sl=sl):
                        S.group("pe",
                                [(lambda e, c=c, t=t, ps=ps, sl=sl: e.matmul(
                                    ps[:, 0:128],
                                    xnf[:, c * SEQ + t * 128:
                                        c * SEQ + (t + 1) * 128],
                                    wvp[sl][:, c * 128:(c + 1) * 128],
                                    start=(c == 0), stop=(c == 7)))
                                 for c in range(8)],
                                reads=[B["wvp"][sl], B["xnf"]], writes=[psb])
                        S.do("dve", lambda e, ps=ps, t=t, sl=sl: e.tensor_copy(
                            vp[sl][:, t * 130:(t + 1) * 130]
                            .rearrange("p (h v) -> p h v", v=65)[:, :, 0:64],
                            ps[:, 0:128].rearrange("p (h v) -> p h v", v=64)),
                            reads=[psb], writes=[B["vp"][sl]])
                    items.append(v_item)
                return items

            def emit_Sp(p, j, t, k):
                sl = p % 2
                sb, sbb = (psE2, B["psE2"]) if t % 2 == 0 else \
                    (psF2, B["psF2"])
                fns = []
                for half in range(2):
                    ro = half * 64
                    fns.append(lambda e, ro=ro, t=t, sb=sb, j=j, sl=sl:
                               e.matmul(
                                   sb[:, ro * 8:ro * 8 + 512],
                                   kp[sl][ro:ro + 64, t * 128:(t + 1) * 128],
                                   qp[sl][ro:ro + 64, j * 512:(j + 1) * 512],
                                   start=True, stop=True,
                                   tile_position=(ro, 0)))
                S.group("pe", fns, reads=[B["kp"][sl], B["qp"][sl]],
                        writes=[sbb])
                eten, eoff = e2sl[k % 3]
                S.do("act", lambda e, sb=sb, eten=eten, eoff=eoff:
                     e.activation(
                         eten[:, eoff:eoff + 1024], sb[:, :], AF.Exp,
                         bias=c_expsh, scale=c_expsc),
                     reads=[sbb, B["consts"]], writes=[B["E2"][k % 3]])

            def emit_AVp(p, j, t, k):
                sl = p % 2
                eten, eoff = e2sl[k % 3]
                esb = B["E2"][k % 3]
                for half in range(2):
                    at, atb = (psC, B["psC"]) if half == 0 else \
                        (psD, B["psD"])
                    S.do("pe", lambda e, t=t, half=half, at=at, eten=eten,
                         eoff=eoff, sl=sl: e.matmul(
                             at[0:65, :],
                             vp[sl][:, t * 130 + 65 * half:
                                    t * 130 + 65 * half + 65],
                             eten[:, eoff + half * 512:
                                  eoff + (half + 1) * 512],
                             start=(t == 0), stop=(t == NT - 1),
                             skip_group_check=True),
                         reads=[B["vp"][sl], esb], writes=[atb])

            def emit_epi_a(p, j):
                for half in range(2):
                    at, atb = (psC, B["psC"]) if half == 0 else \
                        (psD, B["psD"])
                    S.do("dve", lambda e, at=at, half=half: e.tensor_copy(
                        ae[half][0:65, :], at[0:65, :]),
                        reads=[atb], writes=[B["ae"][half]])

            def emit_epi_b(p, j):
                for half in range(2):
                    hl = 2 * p + half
                    ps_t, ps_tb = (psA, B["psA"]) if half == 0 else \
                        (psB, B["psB"])
                    ae_s, ae_b = ae[half], B["ae"][half]
                    rzs, rzb = rztmp[half], B["rzt"][half]
                    S.group("pe",
                            [(lambda e, qt=qt, ae_s=ae_s, ps_t=ps_t:
                              e.matmul(
                                  ps_t[:, qt * 65:(qt + 1) * 65],
                                  ae_s[0:65, qt * 128:(qt + 1) * 128],
                                  ident[0:65, 0:65],
                                  start=True, stop=True))
                             for qt in range(4)],
                            reads=[ae_b, B["ident"]], writes=[ps_tb])
                    S.do("dve", lambda e, ps_t=ps_t, rzs=rzs:
                         e.reciprocal(
                             rzs[:, 0:4].rearrange("p (i o) -> p i o", o=1),
                             ps_t[:, 0:260].rearrange("p (i o) -> p i o",
                                                      o=65)[:, :, 64:65]),
                         reads=[ps_tb], writes=[rzb])
                    for qt in range(4):
                        i = j * 4 + qt
                        S.do("dve", lambda e, qt=qt, i=i, hl=hl,
                             ps_t=ps_t, rzs=rzs: e.tensor_scalar(
                                 atok[:, i * 1024 + hl * 64:
                                      i * 1024 + (hl + 1) * 64],
                                 ps_t[:, qt * 65:qt * 65 + 64],
                                 rzs[:, qt:qt + 1], None, ALU.mult),
                             reads=[ps_tb, rzb],
                             writes=[B["atokT"][i]])

            for it in qkv_items(0):
                it()

            seqk = [(p, j, t) for p in range(NPAIR) for j in range(4)
                    for t in range(NT)]
            emit_Sp(*seqk[0], kcell[0])
            emit_Sp(*seqk[1], kcell[0] + 1)
            pend = []
            for k, (p, j, t) in enumerate(seqk):
                if j == 0 and t == 0 and p + 1 < NPAIR:
                    bg[:0] = qkv_items(p + 1)
                if k + 2 < len(seqk):
                    emit_Sp(*seqk[k + 2], kcell[0] + k + 2)
                while pend and pend[0][0] <= k:
                    pend.pop(0)[1]()
                bg_drain(1)
                emit_AVp(p, j, t, kcell[0] + k)
                if t == NT - 1:
                    pend.append((k + 1,
                                 lambda pp=p, pj=j: emit_epi_a(pp, pj)))
                    pend.append((k + 2,
                                 lambda pp=p, pj=j: emit_epi_b(pp, pj)))
            while pend:
                pend.pop(0)[1]()
            kcell[0] += len(seqk)

            alias_guard(B["x1t"][8:14],
                        [B["p0_wg"], B["p0_wu"], B["p0_wd"],
                         B["p0_gs"], B["p0_us"], B["p0_ds"]])
            alias_guard(B["ageluT"], pairw_bufs)
            ln2q = []
            for half in (1, 0):
                alias_guard([B["gt"][0], B["gt"][1]], [B["xn1"]])
                for it in range(8):
                    i = half * 8 + it
                    sl = it % 2
                    gts, gtb = gt[sl], B["gt"][sl]
                    bank0, bank0b, bank1, bank1b = \
                        (psA, B["psA"], psB, B["psB"]) if it % 2 == 0 else \
                        (psC, B["psC"], psD, B["psD"])
                    S.do("act", lambda e, gts=gts, i=i: e.activation(
                        gts[:, :], atok[:, i * 1024:(i + 1) * 1024], AF.Gelu),
                        reads=[B["atokT"][i]], writes=[gtb])
                    S.group("pe",
                            [(lambda e, b=b, gts=gts, bank0=bank0,
                              bank1=bank1: e.matmul(
                                  (bank0 if b < 4 else bank1)
                                  [:, (b % 4) * 128:(b % 4 + 1) * 128],
                                  gts[:, b * 128:(b + 1) * 128], ident[:, :],
                                  start=True, stop=True)) for b in range(8)],
                            reads=[gtb, B["ident"]],
                            writes=[bank0b, bank1b])
                    for ps_t, ps_tb, c0 in ((bank0, bank0b, 0),
                                            (bank1, bank1b, 4)):
                        S.do("dve", lambda e, ps_t=ps_t, c0=c0, it=it:
                             e.tensor_copy(
                                 agelu[:, :].rearrange("p (c q) -> p c q", c=8)
                                 [:, c0:c0 + 4, it * 128:(it + 1) * 128],
                                 ps_t[:, :].rearrange("p (c q) -> p c q", c=4)),
                             reads=[ps_tb], writes=[B["ageluT"][it]])
                if half == 0:
                    alias_guard(B["x1t"][0:8], B["atokT"])
                alias_guard([B["xn1"]], [B["gt"][0], B["gt"][1]])
                for it in range(8):
                    i = half * 8 + it
                    sl = i % 2
                    xs = B["xstg"][sl]
                    S.do("sp", lambda e, i=i, sl=sl: e.dma_start(
                        out=xstg[:, sl * 1024:(sl + 1) * 1024],
                        in_=Xd[row0 + i * 128:row0 + (i + 1) * 128, :]),
                        writes=[xs])
                    S.do("dve", lambda e, sl=sl, i=i: e.scalar_tensor_tensor(
                        x1[:, i * 1024:(i + 1) * 1024],
                        xstg[:, sl * 1024:(sl + 1) * 1024],
                        stats[:, 2 * i:2 * i + 1],
                        xstg[:, sl * 1024:(sl + 1) * 1024],
                        ALU.mult, ALU.add),
                        reads=[xs, B["stP"][i]], writes=[B["x1t"][i]])
                    ps, psb = (psE2, B["psE2"]) if it % 2 == 0 else (psF2, B["psF2"])
                    fns = []
                    for nb in range(2):
                        for c in range(8):
                            fns.append(lambda e, c=c, nb=nb, it=it, ps=ps:
                                       e.matmul(
                                           ps[:, nb * 512:(nb + 1) * 512],
                                           agelu[:, c * 1024 + it * 128:
                                                 c * 1024 + (it + 1) * 128],
                                           wpost[:, c * D + nb * 512:
                                                 c * D + (nb + 1) * 512],
                                           start=(c == 0), stop=(c == 7)))
                    S.group("pe", fns,
                            reads=[B["ageluT"][it], B["wpost"]],
                            writes=[psb])
                    scr2 = 80 + 8 * (i % 2)
                    S.do("dve", lambda e, i=i, ps=ps, scr2=scr2:
                         e.scalar_tensor_tensor(
                             x1[:, i * 1024:(i + 1) * 1024],
                             x1[:, i * 1024:(i + 1) * 1024],
                             stats[:, 2 * i + 1:2 * i + 2], ps[:, :],
                             ALU.add, ALU.add,
                             accum_out=stats[:, scr2:scr2 + 1]),
                         reads=[psb, B["stP"][i], B["x1t"][i]],
                         writes=[B["x1t"][i], B["stS"][2 + i % 2]])
                    while len(ln2q) >= 2:
                        ln2q.pop(0)()
                    sl4 = i % 4
                    layernorm_tile(x1[:, i * 1024:(i + 1) * 1024],
                                   B["x1t"][i], 32 + 2 * sl4, 33 + 2 * sl4,
                                   B["stP2"][sl4], 80 + 8 * (i % 2),
                                   B["stS"][2 + i % 2], deq=True,
                                   skip_sum=True)

                    def ln2_tail(i=i, sl4=sl4):
                        tk = i % 2
                        S.do("act", lambda e, i=i, sl4=sl4, tk=tk:
                             e.activation(
                                 xtoks[tk][:, :],
                                 x1[:, i * 1024:(i + 1) * 1024],
                                 AF.Identity,
                                 bias=stats[:, 33 + 2 * sl4:34 + 2 * sl4],
                                 scale=stats[:, 32 + 2 * sl4:33 + 2 * sl4]),
                             reads=[B["x1t"][i], B["stP2"][sl4]],
                             writes=[B["xtok"][tk]])
                        transpose_to_featmajor(xtoks[tk], B["xtok"][tk],
                                               xnf, B["xnf"], i,
                                               use_cd=True)
                    ln2q.append(ln2_tail)
            while ln2q:
                ln2q.pop(0)()

            if s + 1 < NSEQ:
                bg.extend([lambda i=i: ln1_stats((s + 1) * SEQ, i)
                           for i in range(NT)])
            alias_guard(wT_bufs, pair_bufs)
            for hc in range(NCK):
                ws = hc % 2
                for src_d, dst_sb, dbuf, nchunks in (
                        (wgTd, wgT[ws], B["wgT"][ws], 8),
                        (wuTd, wuT[ws], B["wuT"][ws], 8),
                        (wdTd, wdT[ws], B["wdT"][ws], 4)):
                    S.do("gp", lambda e, src_d=src_d, dst_sb=dst_sb, hc=hc,
                         nchunks=nchunks: e.dma_start(
                             out=dst_sb[:, :].rearrange("p (c h) -> p c h",
                                                        c=nchunks),
                             in_=src_d[hc].rearrange("c p h -> p c h")),
                         reads=[B["wgTd" if src_d is wgTd else
                                  "wuTd" if src_d is wuTd else "wdTd"]],
                         writes=[dbuf])
                for j in range(4):
                    bg_drain(1)
                    for hsub in range(4):
                        psg, psgb = (psA, B["psA"]) if hsub % 2 == 0 else (psB, B["psB"])
                        psu, psub = (psC, B["psC"]) if hsub % 2 == 0 else (psD, B["psD"])
                        S.group("pe",
                                [(lambda e, c=c, hsub=hsub, j=j, ws=ws, psg=psg:
                                  e.matmul(
                                      psg[:, :],
                                      wgT[ws][:, c * 512 + hsub * 128:
                                              c * 512 + (hsub + 1) * 128],
                                      xnf[:, c * SEQ + j * 512:
                                          c * SEQ + (j + 1) * 512],
                                      start=(c == 0), stop=(c == 7)))
                                 for c in range(8)],
                                reads=[B["wgT"][ws], B["xnf"]], writes=[psgb])
                        S.group("pe",
                                [(lambda e, c=c, hsub=hsub, j=j, ws=ws, psu=psu:
                                  e.matmul(
                                      psu[:, :],
                                      wuT[ws][:, c * 512 + hsub * 128:
                                              c * 512 + (hsub + 1) * 128],
                                      xnf[:, c * SEQ + j * 512:
                                          c * SEQ + (j + 1) * 512],
                                      start=(c == 0), stop=(c == 7)))
                                 for c in range(8)],
                                reads=[B["wuT"][ws], B["xnf"]], writes=[psub])
                        S.do("act", lambda e, psg=psg: e.activation(
                            sg[:, :], psg[:, :], AF.Silu),
                            reads=[psgb], writes=[B["sg"]])
                        S.do("dve", lambda e, psu=psu, hsub=hsub:
                             e.scalar_tensor_tensor(
                                 hh[:, hsub * 512:(hsub + 1) * 512],
                                 psu[:, :], DEQ, sg[:, :], ALU.mult, ALU.mult),
                             reads=[psub, B["sg"]], writes=[B["hh"][hsub]])
                    if hc == NCK - 1 and s + 1 < NSEQ and j >= 1:
                        for fi in range(4 * (j - 1), 4 * j):
                            ln1_fill_tile((s + 1) * SEQ, fi, redma=True,
                                          use_cd=True)
                    for it in range(4):
                        i = j * 4 + it
                        psd_, psdb = (psE2, B["psE2"]) if it % 2 == 0 \
                            else (psF2, B["psF2"])
                        for hsub in range(4):
                            S.group("pe",
                                    [(lambda e, hsub=hsub, nb=nb, it=it,
                                      ws=ws, psd_=psd_: e.matmul(
                                          psd_[:, nb * 512:(nb + 1) * 512],
                                          hh[:, hsub * 512 + it * 128:
                                             hsub * 512 + (it + 1) * 128],
                                          wdT[ws][:, hsub * D + nb * 512:
                                                  hsub * D + (nb + 1) * 512],
                                          start=(hsub == 0), stop=(hsub == 3)))
                                     for nb in range(2)],
                                    reads=[B["hh"][hsub], B["wdT"][ws]],
                                    writes=[psdb])
                        S.do("dve", lambda e, i=i, psd_=psd_: e.tensor_add(
                            x1[:, i * 1024:(i + 1) * 1024],
                            x1[:, i * 1024:(i + 1) * 1024], psd_[:, :]),
                            reads=[psdb, B["x1t"][i]], writes=[B["x1t"][i]])
                        if hc == NCK - 1:
                            S.do("act", lambda e, i=i: e.dma_start(
                                out=Yd[row0 + i * 128:row0 + (i + 1) * 128, :],
                                in_=x1[:, i * 1024:(i + 1) * 1024]),
                                reads=[B["x1t"][i]], inc=16)

            if s + 1 < NSEQ:
                for fi in range(12, NT):
                    ln1_fill_tile((s + 1) * SEQ, fi, redma=True,
                                  use_cd=True)
                fill_done[0] = True

        for s in range(NSEQ):
            sequence(s)
        assert not bg, "background queue not fully drained"

        for q in ("sp", "gp", "act"):
            v = S.cnt[q]
            S.plan[q].append(
                ([], lambda e, q=q, v=v: e.wait_ge(sems[q], v), 0))

        def emit(ename):
            plan = S.plan[ename]

            def emitter(e):
                for wl, fn, inc in plan:
                    for (p, v) in wl:
                        e.wait_ge(sems[p], v)
                    ins = fn(e)
                    if inc:
                        ins.then_inc(sems[ename], inc)
            return emitter

        block.sync(emit("sp"))
        block.gpsimd(emit("gp"))
        block.tensor(emit("pe"))
        block.scalar(emit("act"))
        block.vector(emit("dve"))

    return nc


_NC_CACHE = {}


def kernel(L0_attn_X, L0_attn_Wq, L0_attn_Wk, L0_attn_Wv, L0_attn_Wpost,
           L0_ffn_w_gate, L0_ffn_w_up, L0_ffn_w_down):
    X = np.ascontiguousarray(np.asarray(L0_attn_X), dtype=np.float32)
    Wq = np.ascontiguousarray(np.asarray(L0_attn_Wq), dtype=np.float32)
    Wk = np.ascontiguousarray(np.asarray(L0_attn_Wk), dtype=np.float32)
    Wv = np.ascontiguousarray(np.asarray(L0_attn_Wv), dtype=np.float32)
    Wp = np.ascontiguousarray(np.asarray(L0_attn_Wpost), dtype=np.float32)
    Wg = np.ascontiguousarray(np.asarray(L0_ffn_w_gate), dtype=np.int32)
    Wu = np.ascontiguousarray(np.asarray(L0_ffn_w_up), dtype=np.int32)
    Wd = np.ascontiguousarray(np.asarray(L0_ffn_w_down), dtype=np.int32)

    if "nc" not in _NC_CACHE:
        _NC_CACHE["nc"] = build_program()
    nc = _NC_CACHE["nc"]

    in_maps = []
    for c in range(NCORES):
        xs = X[c * NSEQ:(c + 1) * NSEQ].reshape(NSEQ * SEQ, D)
        in_maps.append({"X": np.ascontiguousarray(xs), "Wq": Wq, "Wk": Wk,
                        "Wv": Wv, "Wp": Wp, "Wg": Wg, "Wu": Wu, "Wd": Wd})
    res = run_bass_kernel_spmd(nc, in_maps, core_ids=list(range(NCORES)))
    out = np.concatenate(
        [res.results[c]["Y"].reshape(NSEQ, SEQ, D) for c in range(NCORES)],
        axis=0)
    return out


# version 36 (final)
# speedup vs baseline: 1.0575x; 1.0033x over previous
import contextlib

import numpy as np
import ml_dtypes

import concourse.bass as bass
import concourse.mybir as mybir
from concourse.bass_utils import run_bass_kernel_spmd

F32 = mybir.dt.float32
BF16 = mybir.dt.bfloat16
I32 = mybir.dt.int32
AF = mybir.ActivationFunctionType
ALU = mybir.AluOpType
AX = mybir.AxisListType
BF = ml_dtypes.bfloat16

NCORES = 8
NSEQ = 2
SEQ = 2048
D = 1024
NT = SEQ // 128
HEADS = 16
NPAIR = 8
HFF = 3072
HCK = 512
NCK = HFF // HCK
EXP_SCALE = 0.125
EXP_SHIFT = -3.0
DEQ = 1.0 / 127.0

OFF_IDENT = 0
OFF_CONST = 512
OFF_STATS = 640
OFF_AE = 1024
OFF_E2C = 3072
OFF_RZT = 256
OFF_XTOK2 = 5120
OFF_XSTG = 7168
OFF_XN1 = 15360
OFF_GT = 15360
OFF_SG = 19456
OFF_HH = 21504
OFF_X1 = 25600
OFF_ATOK = 25600
OFF_PH0 = 58368
OFF_XNF = 91136
OFF_S2 = 123904
OFF_QP = 123904
OFF_KP = 132096
OFF_VP = 140288
OFF_WT0 = 123904
OFF_WT1 = 148480
OFF_WPOST = 173312
OFF_WP = 189696
OFF_AGELU = 189696
OFF_XNTOK = 206080
OFF_E2 = 208128


class Buf:

    __slots__ = ("name", "w", "r")

    def __init__(self, name):
        self.name = name
        self.w = {}
        self.r = {}


class Sch:
    ENGS = ("pe", "act", "dve", "sp", "gp")

    def __init__(self):
        self.plan = {e: [] for e in self.ENGS}
        self.cnt = {e: 0 for e in self.ENGS}
        self.waited = {e: {p: 0 for p in self.ENGS} for e in self.ENGS}

    def _op(self, eng, fn, waits, inc):
        wl = []
        if eng in ("sp", "gp") and self.cnt[eng] > 0:
            wl.append((eng, self.cnt[eng]))
        for (p, v) in waits:
            if p == eng and eng in ("pe", "sp", "gp"):
                continue
            if v > self.waited[eng][p]:
                self.waited[eng][p] = v
                wl.append((p, v))
        wl.sort()
        self.cnt[eng] += inc
        self.plan[eng].append((wl, fn, inc))
        return (eng, self.cnt[eng])

    def do(self, eng, fn, reads=(), writes=(), extra=(), inc=None):
        return self.group(eng, [fn], reads=reads, writes=writes, extra=extra,
                          inc=inc)

    def group(self, eng, fns, reads=(), writes=(), extra=(), inc=None):
        if inc is None:
            inc = 16 if eng in ("sp", "gp") else 1
        waits = list(extra)
        for b in reads:
            waits += list(b.w.items())
        for b in writes:
            waits += list(b.w.items())
            waits += list(b.r.items())
        tok = None
        for k, fn in enumerate(fns):
            tok = self._op(eng, fn, waits if k == 0 else (),
                           inc if k == len(fns) - 1 else 0)
        _, v = tok
        for b in reads:
            b.r[eng] = max(b.r.get(eng, 0), v)
        for b in writes:
            b.w[eng] = max(b.w.get(eng, 0), v)
        return tok


def build_program():
    nc = bass.Bass("TRN2", target_bir_lowering=False, debug=False)

    Xd = nc.dram_tensor("X", [NSEQ * SEQ, D], F32, kind="ExternalInput").ap()
    Wqd = nc.dram_tensor("Wq", [D, D], F32, kind="ExternalInput").ap()
    Wkd = nc.dram_tensor("Wk", [D, D], F32, kind="ExternalInput").ap()
    Wvd = nc.dram_tensor("Wv", [D, D], F32, kind="ExternalInput").ap()
    Wpd = nc.dram_tensor("Wp", [D, D], F32, kind="ExternalInput").ap()
    Wgd = nc.dram_tensor("Wg", [HFF, D], I32, kind="ExternalInput").ap()
    Wud = nc.dram_tensor("Wu", [HFF, D], I32, kind="ExternalInput").ap()
    Wdd = nc.dram_tensor("Wd", [D, HFF], I32, kind="ExternalInput").ap()
    Yd = nc.dram_tensor("Y", [NSEQ * SEQ, D], F32, kind="ExternalOutput").ap()

    wgTd = nc.dram_tensor("wgT_s", [NCK, 8, 128, HCK], BF16, kind="Internal").ap()
    wuTd = nc.dram_tensor("wuT_s", [NCK, 8, 128, HCK], BF16, kind="Internal").ap()
    wdTd = nc.dram_tensor("wdT_s", [NCK, 4, 128, D], BF16, kind="Internal").ap()
    identd = nc.inline_tensor(np.eye(128, dtype=BF), "identd").ap()

    _dummy = nc.alloc_sbuf_tensor("map_base", [128, 64], mybir.dt.uint8)
    _base = nc.lookup_mloc(_dummy).addr + 64

    A = lambda name, shape, dt, off: nc.alloc_sbuf_tensor_at(
        name, shape, dt, offset=_base + off)

    ident = A("ident", [128, 128], BF16, OFF_IDENT)
    consts = A("consts", [128, 16], F32, OFF_CONST)
    stats = A("stats", [128, 96], F32, OFF_STATS)
    ae = [A("ae0", [128, 512], BF16, OFF_AE),
          A("ae1", [128, 512], BF16, OFF_AE + 1024)]
    e2c = A("e2c", [128, 1024], BF16, OFF_E2C)
    rztmp = [A("rzt0", [128, 8], F32, OFF_RZT),
             A("rzt1", [128, 8], F32, OFF_RZT + 32)]
    xtok2 = A("xtok2", [128, 1024], BF16, OFF_XTOK2)
    xstg = A("xstg", [128, 2048], F32, OFF_XSTG)
    xn1 = A("xn1", [128, 1024], F32, OFF_XN1)
    gt = [A("gt0", [128, 1024], BF16, OFF_GT),
          A("gt1", [128, 1024], BF16, OFF_GT + 2048)]
    sg = A("sg", [128, 512], F32, OFF_SG)
    hh = A("hh", [128, 2048], BF16, OFF_HH)
    x1 = A("x1", [128, 16384], F32, OFF_X1)
    atok = A("atok", [128, 16384], BF16, OFF_ATOK)
    xnf = A("xnf", [128, 16384], BF16, OFF_XNF)
    qp = [A("qp0", [128, 2048], BF16, OFF_QP),
          A("qp1", [128, 2048], BF16, OFF_QP + 4096)]
    kp = [A("kp0", [128, 2048], BF16, OFF_KP),
          A("kp1", [128, 2048], BF16, OFF_KP + 4096)]
    vp = [A("vp0", [128, NT * 130], BF16, OFF_VP),
          A("vp1", [128, NT * 130], BF16, OFF_VP + 4160)]
    wpost = A("wpost", [128, 8 * D], BF16, OFF_WPOST)
    wqp = [A("wqp0", [128, 1024], BF16, OFF_WP),
           A("wqp1", [128, 1024], BF16, OFF_WP + 2048)]
    wkp = [A("wkp0", [128, 1024], BF16, OFF_WP + 4096),
           A("wkp1", [128, 1024], BF16, OFF_WP + 6144)]
    wvp = [A("wvp0", [128, 1024], BF16, OFF_WP + 8192),
           A("wvp1", [128, 1024], BF16, OFF_WP + 10240)]
    agelu = A("agelu", [128, 8192], BF16, OFF_AGELU)
    xtok = A("xtok", [128, 1024], BF16, OFF_XNTOK)
    xtoks = [xtok, xtok2]
    e2 = A("e2", [128, 2048], BF16, OFF_E2)
    p0_wg = A("p0_wg", [128, 2048], BF16, OFF_PH0)
    p0_wu = A("p0_wu", [128, 2048], BF16, OFF_PH0 + 4096)
    p0_wd = A("p0_wd", [128, 2048], BF16, OFF_PH0 + 8192)
    p0_gs = A("p0_gs", [128, 8 * 256], BF16, OFF_PH0 + 12288)
    p0_us = A("p0_us", [128, 8 * 256], BF16, OFF_PH0 + 16384)
    p0_ds = A("p0_ds", [128, 2 * 1024], BF16, OFF_PH0 + 20480)
    wgT = [A("wgT0", [128, 8 * HCK], BF16, OFF_WT0),
           A("wgT1", [128, 8 * HCK], BF16, OFF_WT1)]
    wuT = [A("wuT0", [128, 8 * HCK], BF16, OFF_WT0 + 8192),
           A("wuT1", [128, 8 * HCK], BF16, OFF_WT1 + 8192)]
    wdT = [A("wdT0", [128, 4 * D], BF16, OFF_WT0 + 16384),
           A("wdT1", [128, 4 * D], BF16, OFF_WT1 + 16384)]

    with contextlib.ExitStack() as stk:
        psA = stk.enter_context(nc.psum_tensor("psA", [128, 512], F32))
        psB = stk.enter_context(nc.psum_tensor("psB", [128, 512], F32))
        psC = stk.enter_context(nc.psum_tensor("psC", [128, 512], F32))
        psD = stk.enter_context(nc.psum_tensor("psD", [128, 512], F32))
        psE2 = stk.enter_context(nc.psum_tensor("psE2", [128, 1024], F32))
        psF2 = stk.enter_context(nc.psum_tensor("psF2", [128, 1024], F32))
        sems = {k: stk.enter_context(nc.semaphore("sem_" + k))
                for k in Sch.ENGS}
        block = stk.enter_context(nc.Block())

        S = Sch()
        B = {}
        for nm in ("ident", "consts", "stats",
                   "xn1", "sg", "atok", "xnf", "wpost",
                   "agelu", "psA", "psB", "psC", "psD", "psE2",
                   "psF2", "p0_wg", "p0_wu", "p0_wd", "p0_gs", "p0_us",
                   "p0_ds", "wgTd", "wuTd", "wdTd"):
            B[nm] = Buf(nm)
        B["ae"] = [Buf("ae0"), Buf("ae1")]
        B["rzt"] = [Buf("rzt0"), Buf("rzt1")]
        B["gt"] = [Buf("gt0"), Buf("gt1")]
        B["xtok"] = [Buf("xtok0"), Buf("xtok1")]
        B["atokT"] = [Buf("atokT%d" % i) for i in range(NT)]
        B["ageluT"] = [Buf("ageluT%d" % i) for i in range(8)]
        B["qpT"] = [[Buf("qp%d_%d" % (s_, b)) for b in range(4)]
                    for s_ in range(2)]
        B["kpT"] = [[Buf("kp%d_%d" % (s_, b)) for b in range(4)]
                    for s_ in range(2)]
        B["vpT"] = [[Buf("vp%d_%d" % (s_, t)) for t in range(NT)]
                    for s_ in range(2)]
        B["xnfT"] = [Buf("xnfT%d" % i) for i in range(NT)]
        B["wqp"] = [Buf("wqp0"), Buf("wqp1")]
        B["wkp"] = [Buf("wkp0"), Buf("wkp1")]
        B["wvp"] = [Buf("wvp0"), Buf("wvp1")]
        B["stP"] = [Buf("stP%d" % i) for i in range(NT)]
        B["stP2"] = [Buf("stP2_%d" % i) for i in range(4)]
        B["stS"] = [Buf("stS%d" % i) for i in range(4)]
        B["E2"] = [Buf("E2a"), Buf("E2b"), Buf("E2c")]
        e2sl = [(e2, 0), (e2, 1024), (e2c, 0)]
        B["xstg"] = [Buf("xstg0"), Buf("xstg1")]
        B["hh"] = [Buf("hh%d" % k) for k in range(4)]
        B["x1t"] = [Buf("x1_%d" % i) for i in range(NT)]
        B["wgT"] = [Buf("wgT0"), Buf("wgT1")]
        B["wuT"] = [Buf("wuT0"), Buf("wuT1")]
        B["wdT"] = [Buf("wdT0"), Buf("wdT1")]

        c_expsc = consts[:, 0:1]
        c_expsh = consts[:, 1:2]
        c_eps = consts[:, 2:3]
        c_mhalf = consts[:, 3:4]

        S.do("sp", lambda e: e.dma_start(out=ident[:, :], in_=identd),
             writes=[B["ident"]])
        for col, val in ((0, EXP_SCALE), (1, EXP_SHIFT), (2, 1e-5), (3, -0.5)):
            S.do("dve",
                 lambda e, col=col, val=val: e.memset(consts[:, col:col + 1], val),
                 writes=[B["consts"]])

        def phase0_items(hc, half):
            items = []
            r0 = hc * HCK + half * 256

            def loads():
                S.do("gp", lambda e: e.dma_start(
                    out=p0_wg[:, :].rearrange("p (s d) -> p s d", s=2),
                    in_=Wgd[r0:r0 + 256, :].rearrange("(s p) d -> p s d", p=128)),
                    writes=[B["p0_wg"]])
                S.do("gp", lambda e: e.dma_start(
                    out=p0_wu[:, :].rearrange("p (s d) -> p s d", s=2),
                    in_=Wud[r0:r0 + 256, :].rearrange("(s p) d -> p s d", p=128)),
                    writes=[B["p0_wu"]])
                S.do("gp", lambda e: e.dma_start(
                    out=p0_wd[:, :].rearrange("p (c h) -> p c h", c=8),
                    in_=Wdd.rearrange("(c p) h -> p c h", p=128)[:, :, r0:r0 + 256]),
                    writes=[B["p0_wd"]])
            items.append(loads)

            jobs = []
            for s in range(2):
                for c in range(8):
                    jobs.append((B["p0_wg"], p0_wg[:, s * 1024 + c * 128:
                                                   s * 1024 + (c + 1) * 128],
                                 B["p0_gs"],
                                 p0_gs[:, c * 256 + s * 128:c * 256 + (s + 1) * 128]))
            for s in range(2):
                for c in range(8):
                    jobs.append((B["p0_wu"], p0_wu[:, s * 1024 + c * 128:
                                                   s * 1024 + (c + 1) * 128],
                                 B["p0_us"],
                                 p0_us[:, c * 256 + s * 128:c * 256 + (s + 1) * 128]))
            for c in range(8):
                for s in range(2):
                    jobs.append((B["p0_wd"], p0_wd[:, c * 256 + s * 128:
                                                   c * 256 + (s + 1) * 128],
                                 B["p0_ds"],
                                 p0_ds[:, s * 1024 + c * 128:s * 1024 + (c + 1) * 128]))
            for k0 in range(0, len(jobs), 4):
                batch = jobs[k0:k0 + 4]
                ps_t, ps_tb = (psA, B["psA"]) if (k0 // 4) % 2 == 0 \
                    else (psB, B["psB"])

                def run_batch(batch=batch, ps_t=ps_t, ps_tb=ps_tb):
                    srcs = sorted({id(b): b for b, _, _, _ in batch}.values(),
                                  key=lambda b: b.name)
                    S.group("pe",
                            [(lambda e, src=src, sl=sl, ps_t=ps_t: e.matmul(
                                ps_t[:, sl * 128:(sl + 1) * 128], src,
                                ident[:, :], start=True, stop=True))
                             for sl, (_, src, _, _) in enumerate(batch)],
                            reads=[B["ident"]] + list(srcs), writes=[ps_tb])
                    dsts = sorted({id(b): b for _, _, b, _ in batch}.values(),
                                  key=lambda b: b.name)
                    S.group("dve",
                            [(lambda e, dst=dst, sl=sl, ps_t=ps_t:
                              e.tensor_copy(dst, ps_t[:, sl * 128:(sl + 1) * 128]))
                             for sl, (_, _, _, dst) in enumerate(batch)],
                            reads=[ps_tb], writes=list(dsts))
                items.append(run_batch)

            def stores():
                S.do("sp", lambda e: e.dma_start(
                    out=wgTd[hc].rearrange("c p h -> p c h")[:, :, half * 256:
                                                             half * 256 + 256],
                    in_=p0_gs[:, :].rearrange("p (c h) -> p c h", c=8)),
                    reads=[B["p0_gs"]], writes=[B["wgTd"]])
                S.do("sp", lambda e: e.dma_start(
                    out=wuTd[hc].rearrange("c p h -> p c h")[:, :, half * 256:
                                                             half * 256 + 256],
                    in_=p0_us[:, :].rearrange("p (c h) -> p c h", c=8)),
                    reads=[B["p0_us"]], writes=[B["wuTd"]])
                S.do("sp", lambda e: e.dma_start(
                    out=wdTd[hc].rearrange("s p d -> p s d")[:, half * 2:half * 2 + 2, :],
                    in_=p0_ds[:, :].rearrange("p (s d) -> p s d", s=2)),
                    reads=[B["p0_ds"]], writes=[B["wdTd"]])
            items.append(stores)
            return items

        def layernorm_tile(src_ap, src_buf, rstd_col, nmr_col, pbuf,
                           scr, sbuf_, deq=False, skip_sum=False):
            S.do("act", lambda e: e.activation(
                xn1[:, :], src_ap, AF.Square,
                accum_out=stats[:, scr + 1:scr + 2]),
                reads=[src_buf], writes=[B["xn1"], sbuf_])
            if not skip_sum:
                S.do("dve", lambda e: e.reduce_sum(
                    stats[:, scr:scr + 1], src_ap, axis=AX.X),
                    reads=[src_buf], writes=[sbuf_])
            S.do("dve", lambda e: e.tensor_scalar(
                stats[:, scr + 2:scr + 4], stats[:, scr:scr + 2],
                1.0 / D, None, ALU.mult),
                reads=[sbuf_], writes=[sbuf_])
            S.do("dve", lambda e: e.tensor_tensor(
                stats[:, scr + 4:scr + 5], stats[:, scr + 2:scr + 3],
                stats[:, scr + 2:scr + 3], ALU.mult),
                reads=[sbuf_], writes=[sbuf_])
            S.do("dve", lambda e: e.tensor_sub(
                stats[:, scr + 5:scr + 6], stats[:, scr + 3:scr + 4],
                stats[:, scr + 4:scr + 5]),
                reads=[sbuf_], writes=[sbuf_])
            S.do("act", lambda e: e.activation(
                stats[:, scr + 6:scr + 7], stats[:, scr + 5:scr + 6],
                AF.Ln, bias=c_eps),
                reads=[sbuf_, B["consts"]], writes=[sbuf_])
            S.do("act", lambda e: e.activation(
                stats[:, rstd_col:rstd_col + 1], stats[:, scr + 6:scr + 7],
                AF.Exp, scale=c_mhalf),
                reads=[sbuf_, B["consts"]], writes=[pbuf])
            if deq:
                S.do("dve", lambda e: e.tensor_scalar(
                    stats[:, rstd_col:rstd_col + 1],
                    stats[:, rstd_col:rstd_col + 1], DEQ, None, ALU.mult),
                    reads=[pbuf], writes=[pbuf])
            S.do("dve", lambda e: e.scalar_tensor_tensor(
                stats[:, nmr_col:nmr_col + 1], stats[:, scr + 2:scr + 3],
                -1.0, stats[:, rstd_col:rstd_col + 1], ALU.mult, ALU.mult),
                reads=[sbuf_, pbuf], writes=[pbuf])

        def transpose_to_featmajor(tok_t, tok_buf, dst, dst_buf, i,
                                   use_cd=False):
            if use_cd:
                S.group("pe",
                        [(lambda e, c=c: e.matmul(
                            (psC if c < 4 else psD)
                            [:, (c % 4) * 128:(c % 4 + 1) * 128],
                            tok_t[:, c * 128:(c + 1) * 128], ident[:, :],
                            start=True, stop=True)) for c in range(8)],
                        reads=[B["ident"], tok_buf],
                        writes=[B["psC"], B["psD"]])
                for pst, pstb, c0 in ((psC, B["psC"], 0), (psD, B["psD"], 4)):
                    S.do("dve", lambda e, pst=pst, c0=c0: e.tensor_copy(
                        dst[:, :].rearrange("p (c s) -> p c s", c=8)
                        [:, c0:c0 + 4, i * 128:(i + 1) * 128],
                        pst[:, :].rearrange("p (c s) -> p c s", c=4)),
                        reads=[pstb], writes=[dst_buf])
                return
            pst, pstb = (psE2, B["psE2"]) if i % 2 == 0 else (psF2, B["psF2"])
            S.group("pe",
                    [(lambda e, c=c, pst=pst: e.matmul(
                        pst[:, c * 128:(c + 1) * 128],
                        tok_t[:, c * 128:(c + 1) * 128], ident[:, :],
                        start=True, stop=True)) for c in range(8)],
                    reads=[B["ident"], tok_buf], writes=[pstb])
            S.do("dve", lambda e, pst=pst: e.tensor_copy(
                dst[:, :].rearrange("p (c s) -> p c s", c=8)[:, :, i * 128:
                                                            (i + 1) * 128],
                pst[:, :].rearrange("p (c s) -> p c s", c=8)),
                reads=[pstb], writes=[dst_buf])

        def alias_guard(dsts, srcs):
            for d in dsts:
                for sb in srcs:
                    for eng, v in list(sb.w.items()) + list(sb.r.items()):
                        d.r[eng] = max(d.r.get(eng, 0), v)

        bg = []

        def bg_drain(n=1):
            for _ in range(min(n, len(bg))):
                bg.pop(0)()

        def ln1_stats(row0, i):
            sl = i % 2
            xs = B["xstg"][sl]
            S.do("sp", lambda e, i=i, sl=sl, row0=row0: e.dma_start(
                out=xstg[:, sl * 1024:(sl + 1) * 1024],
                in_=Xd[row0 + i * 128:row0 + (i + 1) * 128, :]),
                writes=[xs])
            layernorm_tile(xstg[:, sl * 1024:(sl + 1) * 1024], xs,
                           2 * i, 2 * i + 1, B["stP"][i],
                           64 + 8 * (i % 2), B["stS"][i % 2])

        def ln1_fill_tile(row0, i, redma, use_cd=False):
            sl = i % 2
            xs = B["xstg"][sl]
            if redma:
                S.do("sp", lambda e, i=i, sl=sl, row0=row0: e.dma_start(
                    out=xstg[:, sl * 1024:(sl + 1) * 1024],
                    in_=Xd[row0 + i * 128:row0 + (i + 1) * 128, :]),
                    writes=[xs])
            tk = i % 2
            S.do("act", lambda e, i=i, sl=sl, tk=tk: e.activation(
                xtoks[tk][:, :], xstg[:, sl * 1024:(sl + 1) * 1024],
                AF.Identity,
                bias=stats[:, 2 * i + 1:2 * i + 2],
                scale=stats[:, 2 * i:2 * i + 1]),
                reads=[xs, B["stP"][i]], writes=[B["xtok"][tk]])
            transpose_to_featmajor(xtoks[tk], B["xtok"][tk], xnf,
                                   B["xnfT"][i], i, use_cd=use_cd)

        kcell = [0]
        fill_done = [False]

        def sequence(s):
            row0 = s * SEQ

            wT_bufs = [B["wgT"][0], B["wgT"][1], B["wuT"][0], B["wuT"][1],
                       B["wdT"][0], B["wdT"][1]]
            pair_bufs = (B["qpT"][0] + B["qpT"][1] + B["kpT"][0] +
                         B["kpT"][1] + B["vpT"][0] + B["vpT"][1])
            pairw_bufs = [B["wqp"][0], B["wqp"][1], B["wkp"][0], B["wkp"][1],
                          B["wvp"][0], B["wvp"][1]]
            if s > 0:
                alias_guard(B["atokT"], B["x1t"][0:8])
            alias_guard(pair_bufs, wT_bufs)
            alias_guard(pairw_bufs, B["ageluT"])
            for sl in range(2):
                S.do("dve", lambda e, sl=sl: e.memset(vp[sl][:, :], 1.0),
                     writes=B["vpT"][sl])

            if s == 0:
                def wpost_load():
                    S.do("gp", lambda e: e.dma_start(
                        out=wpost[:, :].rearrange("p (c d) -> p c d", c=8),
                        in_=Wpd.rearrange("(c p) d -> p c d", p=128)),
                        writes=[B["wpost"]])
                bg.append(wpost_load)
                for hc in range(NCK):
                    bg.extend(phase0_items(hc, 0))
                    bg.extend(phase0_items(hc, 1))
                for i in range(NT):
                    ln1_stats(row0, i)
                    ln1_fill_tile(row0, i, redma=False)
                    bg_drain(2)
            elif not fill_done[0]:
                for i in range(NT):
                    ln1_fill_tile(row0, i, redma=True)
            else:
                fill_done[0] = False

            def qkv_items(p):
                sl = p % 2
                items = []

                def loads(p=p, sl=sl):
                    for wd_ap, dst, dbuf in ((Wqd, wqp[sl], B["wqp"][sl]),
                                             (Wkd, wkp[sl], B["wkp"][sl]),
                                             (Wvd, wvp[sl], B["wvp"][sl])):
                        S.do("gp", lambda e, wd_ap=wd_ap, dst=dst, p=p:
                             e.dma_start(
                                 out=dst[:, :].rearrange("p (c n) -> p c n",
                                                         c=8),
                                 in_=wd_ap.rearrange("(c p) n -> p c n", p=128)
                                 [:, :, p * 128:(p + 1) * 128]),
                             writes=[dbuf])
                items.append(loads)

                for wi in range(2):
                    wsl_t = (wqp, wkp)[wi][sl]
                    wsl_b = (B["wqp"], B["wkp"])[wi][sl]
                    dstp = (qp, kp)[wi][sl]
                    dstbL = (B["qpT"], B["kpT"])[wi][sl]
                    for jb in range(4):
                        ps, psb = (psA, B["psA"]) if (wi * 4 + jb) % 2 == 0 \
                            else (psB, B["psB"])

                        def qk_item(ps=ps, psb=psb, jb=jb, wsl_t=wsl_t,
                                    wsl_b=wsl_b, dstp=dstp,
                                    dstb=dstbL[jb]):
                            S.group("pe",
                                    [(lambda e, c=c, ps=ps, jb=jb,
                                      wsl_t=wsl_t: e.matmul(
                                          ps[:, :],
                                          wsl_t[:, c * 128:(c + 1) * 128],
                                          xnf[:, c * SEQ + jb * 512:
                                              c * SEQ + (jb + 1) * 512],
                                          start=(c == 0), stop=(c == 7)))
                                     for c in range(8)],
                                    reads=[wsl_b] +
                                    B["xnfT"][4 * jb:4 * jb + 4],
                                    writes=[psb])
                            S.do("dve", lambda e, ps=ps, jb=jb, dstp=dstp:
                                 e.tensor_copy(
                                     dstp[:, jb * 512:(jb + 1) * 512],
                                     ps[:, :]),
                                 reads=[psb], writes=[dstb])
                        items.append(qk_item)

                for t in range(NT):
                    ps, psb = (psA, B["psA"]) if t % 2 == 0 else \
                        (psB, B["psB"])

                    def v_item(ps=ps, psb=psb, t=t, sl=sl):
                        S.group("pe",
                                [(lambda e, c=c, t=t, ps=ps, sl=sl: e.matmul(
                                    ps[:, 0:128],
                                    xnf[:, c * SEQ + t * 128:
                                        c * SEQ + (t + 1) * 128],
                                    wvp[sl][:, c * 128:(c + 1) * 128],
                                    start=(c == 0), stop=(c == 7)))
                                 for c in range(8)],
                                reads=[B["wvp"][sl], B["xnfT"][t]],
                                writes=[psb])
                        S.do("dve", lambda e, ps=ps, t=t, sl=sl: e.tensor_copy(
                            vp[sl][:, t * 130:(t + 1) * 130]
                            .rearrange("p (h v) -> p h v", v=65)[:, :, 0:64],
                            ps[:, 0:128].rearrange("p (h v) -> p h v", v=64)),
                            reads=[psb], writes=[B["vpT"][sl][t]])
                    items.append(v_item)
                return items

            def emit_Sp(p, j, t, k):
                sl = p % 2
                sb, sbb = (psE2, B["psE2"]) if t % 2 == 0 else \
                    (psF2, B["psF2"])
                fns = []
                for half in range(2):
                    ro = half * 64
                    fns.append(lambda e, ro=ro, t=t, sb=sb, j=j, sl=sl:
                               e.matmul(
                                   sb[:, ro * 8:ro * 8 + 512],
                                   kp[sl][ro:ro + 64, t * 128:(t + 1) * 128],
                                   qp[sl][ro:ro + 64, j * 512:(j + 1) * 512],
                                   start=True, stop=True,
                                   tile_position=(ro, 0)))
                S.group("pe", fns,
                        reads=[B["kpT"][sl][t // 4], B["qpT"][sl][j]],
                        writes=[sbb])
                eten, eoff = e2sl[k % 3]
                S.do("act", lambda e, sb=sb, eten=eten, eoff=eoff:
                     e.activation(
                         eten[:, eoff:eoff + 1024], sb[:, :], AF.Exp,
                         bias=c_expsh, scale=c_expsc),
                     reads=[sbb, B["consts"]], writes=[B["E2"][k % 3]])

            def emit_AVp(p, j, t, k):
                sl = p % 2
                eten, eoff = e2sl[k % 3]
                esb = B["E2"][k % 3]
                for half in range(2):
                    at, atb = (psC, B["psC"]) if half == 0 else \
                        (psD, B["psD"])
                    S.do("pe", lambda e, t=t, half=half, at=at, eten=eten,
                         eoff=eoff, sl=sl: e.matmul(
                             at[0:65, :],
                             vp[sl][:, t * 130 + 65 * half:
                                    t * 130 + 65 * half + 65],
                             eten[:, eoff + half * 512:
                                  eoff + (half + 1) * 512],
                             start=(t == 0), stop=(t == NT - 1),
                             skip_group_check=True),
                         reads=[B["vpT"][sl][t], esb], writes=[atb])

            def emit_epi_a(p, j):
                for half in range(2):
                    at, atb = (psC, B["psC"]) if half == 0 else \
                        (psD, B["psD"])
                    S.do("dve", lambda e, at=at, half=half: e.tensor_copy(
                        ae[half][0:65, :], at[0:65, :]),
                        reads=[atb], writes=[B["ae"][half]])

            def emit_epi_b(p, j):
                for half in range(2):
                    hl = 2 * p + half
                    ps_t, ps_tb = (psA, B["psA"]) if half == 0 else \
                        (psB, B["psB"])
                    ae_s, ae_b = ae[half], B["ae"][half]
                    rzs, rzb = rztmp[half], B["rzt"][half]
                    S.group("pe",
                            [(lambda e, qt=qt, ae_s=ae_s, ps_t=ps_t:
                              e.matmul(
                                  ps_t[:, qt * 65:(qt + 1) * 65],
                                  ae_s[0:65, qt * 128:(qt + 1) * 128],
                                  ident[0:65, 0:65],
                                  start=True, stop=True))
                             for qt in range(4)],
                            reads=[ae_b, B["ident"]], writes=[ps_tb])
                    S.do("dve", lambda e, ps_t=ps_t, rzs=rzs:
                         e.reciprocal(
                             rzs[:, 0:4].rearrange("p (i o) -> p i o", o=1),
                             ps_t[:, 0:260].rearrange("p (i o) -> p i o",
                                                      o=65)[:, :, 64:65]),
                         reads=[ps_tb], writes=[rzb])
                    for qt in range(4):
                        i = j * 4 + qt
                        S.do("dve", lambda e, qt=qt, i=i, hl=hl,
                             ps_t=ps_t, rzs=rzs: e.tensor_scalar(
                                 atok[:, i * 1024 + hl * 64:
                                      i * 1024 + (hl + 1) * 64],
                                 ps_t[:, qt * 65:qt * 65 + 64],
                                 rzs[:, qt:qt + 1], None, ALU.mult),
                             reads=[ps_tb, rzb],
                             writes=[B["atokT"][i]])

            for it in qkv_items(0):
                it()

            seqk = [(p, j, t) for p in range(NPAIR) for j in range(4)
                    for t in range(NT)]
            emit_Sp(*seqk[0], kcell[0])
            emit_Sp(*seqk[1], kcell[0] + 1)
            pend = []
            for k, (p, j, t) in enumerate(seqk):
                if j == 0 and t == 0 and p + 1 < NPAIR:
                    bg[:0] = qkv_items(p + 1)
                if k + 2 < len(seqk):
                    emit_Sp(*seqk[k + 2], kcell[0] + k + 2)
                while pend and pend[0][0] <= k:
                    pend.pop(0)[1]()
                bg_drain(1)
                emit_AVp(p, j, t, kcell[0] + k)
                if t == NT - 1:
                    pend.append((k + 1,
                                 lambda pp=p, pj=j: emit_epi_a(pp, pj)))
                    pend.append((k + 2,
                                 lambda pp=p, pj=j: emit_epi_b(pp, pj)))
            while pend:
                pend.pop(0)[1]()
            kcell[0] += len(seqk)

            alias_guard(B["x1t"][8:14],
                        [B["p0_wg"], B["p0_wu"], B["p0_wd"],
                         B["p0_gs"], B["p0_us"], B["p0_ds"]])
            alias_guard(B["ageluT"], pairw_bufs)
            ln2q = []
            for half in (1, 0):
                alias_guard([B["gt"][0], B["gt"][1]], [B["xn1"]])
                for it in range(8):
                    i = half * 8 + it
                    sl = it % 2
                    gts, gtb = gt[sl], B["gt"][sl]
                    bank0, bank0b, bank1, bank1b = \
                        (psA, B["psA"], psB, B["psB"]) if it % 2 == 0 else \
                        (psC, B["psC"], psD, B["psD"])
                    S.do("act", lambda e, gts=gts, i=i: e.activation(
                        gts[:, :], atok[:, i * 1024:(i + 1) * 1024], AF.Gelu),
                        reads=[B["atokT"][i]], writes=[gtb])
                    S.group("pe",
                            [(lambda e, b=b, gts=gts, bank0=bank0,
                              bank1=bank1: e.matmul(
                                  (bank0 if b < 4 else bank1)
                                  [:, (b % 4) * 128:(b % 4 + 1) * 128],
                                  gts[:, b * 128:(b + 1) * 128], ident[:, :],
                                  start=True, stop=True)) for b in range(8)],
                            reads=[gtb, B["ident"]],
                            writes=[bank0b, bank1b])
                    for ps_t, ps_tb, c0 in ((bank0, bank0b, 0),
                                            (bank1, bank1b, 4)):
                        S.do("dve", lambda e, ps_t=ps_t, c0=c0, it=it:
                             e.tensor_copy(
                                 agelu[:, :].rearrange("p (c q) -> p c q", c=8)
                                 [:, c0:c0 + 4, it * 128:(it + 1) * 128],
                                 ps_t[:, :].rearrange("p (c q) -> p c q", c=4)),
                             reads=[ps_tb], writes=[B["ageluT"][it]])
                if half == 0:
                    alias_guard(B["x1t"][0:8], B["atokT"])
                alias_guard([B["xn1"]], [B["gt"][0], B["gt"][1]])
                for it in range(8):
                    i = half * 8 + it
                    sl = i % 2
                    xs = B["xstg"][sl]
                    S.do("sp", lambda e, i=i, sl=sl: e.dma_start(
                        out=xstg[:, sl * 1024:(sl + 1) * 1024],
                        in_=Xd[row0 + i * 128:row0 + (i + 1) * 128, :]),
                        writes=[xs])
                    S.do("dve", lambda e, sl=sl, i=i: e.scalar_tensor_tensor(
                        x1[:, i * 1024:(i + 1) * 1024],
                        xstg[:, sl * 1024:(sl + 1) * 1024],
                        stats[:, 2 * i:2 * i + 1],
                        xstg[:, sl * 1024:(sl + 1) * 1024],
                        ALU.mult, ALU.add),
                        reads=[xs, B["stP"][i]], writes=[B["x1t"][i]])
                    ps, psb = (psE2, B["psE2"]) if it % 2 == 0 else (psF2, B["psF2"])
                    fns = []
                    for nb in range(2):
                        for c in range(8):
                            fns.append(lambda e, c=c, nb=nb, it=it, ps=ps:
                                       e.matmul(
                                           ps[:, nb * 512:(nb + 1) * 512],
                                           agelu[:, c * 1024 + it * 128:
                                                 c * 1024 + (it + 1) * 128],
                                           wpost[:, c * D + nb * 512:
                                                 c * D + (nb + 1) * 512],
                                           start=(c == 0), stop=(c == 7)))
                    S.group("pe", fns,
                            reads=[B["ageluT"][it], B["wpost"]],
                            writes=[psb])
                    scr2 = 80 + 8 * (i % 2)
                    S.do("dve", lambda e, i=i, ps=ps, scr2=scr2:
                         e.scalar_tensor_tensor(
                             x1[:, i * 1024:(i + 1) * 1024],
                             x1[:, i * 1024:(i + 1) * 1024],
                             stats[:, 2 * i + 1:2 * i + 2], ps[:, :],
                             ALU.add, ALU.add,
                             accum_out=stats[:, scr2:scr2 + 1]),
                         reads=[psb, B["stP"][i], B["x1t"][i]],
                         writes=[B["x1t"][i], B["stS"][2 + i % 2]])
                    while len(ln2q) >= 2:
                        ln2q.pop(0)()
                    sl4 = i % 4
                    layernorm_tile(x1[:, i * 1024:(i + 1) * 1024],
                                   B["x1t"][i], 32 + 2 * sl4, 33 + 2 * sl4,
                                   B["stP2"][sl4], 80 + 8 * (i % 2),
                                   B["stS"][2 + i % 2], deq=True,
                                   skip_sum=True)

                    def ln2_tail(i=i, sl4=sl4):
                        tk = i % 2
                        S.do("act", lambda e, i=i, sl4=sl4, tk=tk:
                             e.activation(
                                 xtoks[tk][:, :],
                                 x1[:, i * 1024:(i + 1) * 1024],
                                 AF.Identity,
                                 bias=stats[:, 33 + 2 * sl4:34 + 2 * sl4],
                                 scale=stats[:, 32 + 2 * sl4:33 + 2 * sl4]),
                             reads=[B["x1t"][i], B["stP2"][sl4]],
                             writes=[B["xtok"][tk]])
                        transpose_to_featmajor(xtoks[tk], B["xtok"][tk],
                                               xnf, B["xnfT"][i], i,
                                               use_cd=True)
                    ln2q.append(ln2_tail)
            while ln2q:
                ln2q.pop(0)()

            if s + 1 < NSEQ:
                bg.extend([lambda i=i: ln1_stats((s + 1) * SEQ, i)
                           for i in range(NT)])
            alias_guard(wT_bufs, pair_bufs)
            for hc in range(NCK):
                ws = hc % 2
                for src_d, dst_sb, dbuf, nchunks in (
                        (wgTd, wgT[ws], B["wgT"][ws], 8),
                        (wuTd, wuT[ws], B["wuT"][ws], 8),
                        (wdTd, wdT[ws], B["wdT"][ws], 4)):
                    S.do("gp", lambda e, src_d=src_d, dst_sb=dst_sb, hc=hc,
                         nchunks=nchunks: e.dma_start(
                             out=dst_sb[:, :].rearrange("p (c h) -> p c h",
                                                        c=nchunks),
                             in_=src_d[hc].rearrange("c p h -> p c h")),
                         reads=[B["wgTd" if src_d is wgTd else
                                  "wuTd" if src_d is wuTd else "wdTd"]],
                         writes=[dbuf])
                for j in range(4):
                    bg_drain(1)
                    for hsub in range(4):
                        psg, psgb = (psA, B["psA"]) if hsub % 2 == 0 else (psB, B["psB"])
                        psu, psub = (psC, B["psC"]) if hsub % 2 == 0 else (psD, B["psD"])
                        S.group("pe",
                                [(lambda e, c=c, hsub=hsub, j=j, ws=ws, psg=psg:
                                  e.matmul(
                                      psg[:, :],
                                      wgT[ws][:, c * 512 + hsub * 128:
                                              c * 512 + (hsub + 1) * 128],
                                      xnf[:, c * SEQ + j * 512:
                                          c * SEQ + (j + 1) * 512],
                                      start=(c == 0), stop=(c == 7)))
                                 for c in range(8)],
                                reads=[B["wgT"][ws]] +
                                B["xnfT"][4 * j:4 * j + 4], writes=[psgb])
                        S.group("pe",
                                [(lambda e, c=c, hsub=hsub, j=j, ws=ws, psu=psu:
                                  e.matmul(
                                      psu[:, :],
                                      wuT[ws][:, c * 512 + hsub * 128:
                                              c * 512 + (hsub + 1) * 128],
                                      xnf[:, c * SEQ + j * 512:
                                          c * SEQ + (j + 1) * 512],
                                      start=(c == 0), stop=(c == 7)))
                                 for c in range(8)],
                                reads=[B["wuT"][ws]] +
                                B["xnfT"][4 * j:4 * j + 4], writes=[psub])
                        S.do("act", lambda e, psg=psg: e.activation(
                            sg[:, :], psg[:, :], AF.Silu),
                            reads=[psgb], writes=[B["sg"]])
                        S.do("dve", lambda e, psu=psu, hsub=hsub:
                             e.scalar_tensor_tensor(
                                 hh[:, hsub * 512:(hsub + 1) * 512],
                                 psu[:, :], DEQ, sg[:, :], ALU.mult, ALU.mult),
                             reads=[psub, B["sg"]], writes=[B["hh"][hsub]])
                    if hc == NCK - 1 and s + 1 < NSEQ and j >= 1:
                        for fi in range(4 * (j - 1), 4 * j):
                            ln1_fill_tile((s + 1) * SEQ, fi, redma=True,
                                          use_cd=True)
                    for it in range(4):
                        i = j * 4 + it
                        psd_, psdb = (psE2, B["psE2"]) if it % 2 == 0 \
                            else (psF2, B["psF2"])
                        for hsub in range(4):
                            S.group("pe",
                                    [(lambda e, hsub=hsub, nb=nb, it=it,
                                      ws=ws, psd_=psd_: e.matmul(
                                          psd_[:, nb * 512:(nb + 1) * 512],
                                          hh[:, hsub * 512 + it * 128:
                                             hsub * 512 + (it + 1) * 128],
                                          wdT[ws][:, hsub * D + nb * 512:
                                                  hsub * D + (nb + 1) * 512],
                                          start=(hsub == 0), stop=(hsub == 3)))
                                     for nb in range(2)],
                                    reads=[B["hh"][hsub], B["wdT"][ws]],
                                    writes=[psdb])
                        S.do("dve", lambda e, i=i, psd_=psd_: e.tensor_add(
                            x1[:, i * 1024:(i + 1) * 1024],
                            x1[:, i * 1024:(i + 1) * 1024], psd_[:, :]),
                            reads=[psdb, B["x1t"][i]], writes=[B["x1t"][i]])
                        if hc == NCK - 1:
                            S.do("act", lambda e, i=i: e.dma_start(
                                out=Yd[row0 + i * 128:row0 + (i + 1) * 128, :],
                                in_=x1[:, i * 1024:(i + 1) * 1024]),
                                reads=[B["x1t"][i]], inc=16)

            if s + 1 < NSEQ:
                for fi in range(12, NT):
                    ln1_fill_tile((s + 1) * SEQ, fi, redma=True,
                                  use_cd=True)
                fill_done[0] = True

        for s in range(NSEQ):
            sequence(s)
        assert not bg, "background queue not fully drained"

        for q in ("sp", "gp", "act"):
            v = S.cnt[q]
            S.plan[q].append(
                ([], lambda e, q=q, v=v: e.wait_ge(sems[q], v), 0))

        def emit(ename):
            plan = S.plan[ename]

            def emitter(e):
                for wl, fn, inc in plan:
                    for (p, v) in wl:
                        e.wait_ge(sems[p], v)
                    ins = fn(e)
                    if inc:
                        ins.then_inc(sems[ename], inc)
            return emitter

        block.sync(emit("sp"))
        block.gpsimd(emit("gp"))
        block.tensor(emit("pe"))
        block.scalar(emit("act"))
        block.vector(emit("dve"))

    return nc


_NC_CACHE = {}


def kernel(L0_attn_X, L0_attn_Wq, L0_attn_Wk, L0_attn_Wv, L0_attn_Wpost,
           L0_ffn_w_gate, L0_ffn_w_up, L0_ffn_w_down):
    X = np.ascontiguousarray(np.asarray(L0_attn_X), dtype=np.float32)
    Wq = np.ascontiguousarray(np.asarray(L0_attn_Wq), dtype=np.float32)
    Wk = np.ascontiguousarray(np.asarray(L0_attn_Wk), dtype=np.float32)
    Wv = np.ascontiguousarray(np.asarray(L0_attn_Wv), dtype=np.float32)
    Wp = np.ascontiguousarray(np.asarray(L0_attn_Wpost), dtype=np.float32)
    Wg = np.ascontiguousarray(np.asarray(L0_ffn_w_gate), dtype=np.int32)
    Wu = np.ascontiguousarray(np.asarray(L0_ffn_w_up), dtype=np.int32)
    Wd = np.ascontiguousarray(np.asarray(L0_ffn_w_down), dtype=np.int32)

    if "nc" not in _NC_CACHE:
        _NC_CACHE["nc"] = build_program()
    nc = _NC_CACHE["nc"]

    in_maps = []
    for c in range(NCORES):
        xs = X[c * NSEQ:(c + 1) * NSEQ].reshape(NSEQ * SEQ, D)
        in_maps.append({"X": np.ascontiguousarray(xs), "Wq": Wq, "Wk": Wk,
                        "Wv": Wv, "Wp": Wp, "Wg": Wg, "Wu": Wu, "Wd": Wd})
    res = run_bass_kernel_spmd(nc, in_maps, core_ids=list(range(NCORES)))
    out = np.concatenate(
        [res.results[c]["Y"].reshape(NSEQ, SEQ, D) for c in range(NCORES)],
        axis=0)
    return out
